# version 08
# baseline (speedup 1.0000x reference)
import numpy as np
import ml_dtypes
from contextlib import ExitStack

import concourse.bass as bass
import concourse.mybir as mybir
from concourse.alu_op_type import AluOpType as OP
from concourse.bass_utils import run_bass_kernel_spmd

F32 = mybir.dt.float32
BF16 = mybir.dt.bfloat16
F8 = mybir.dt.float8e4
AF = mybir.ActivationFunctionType
AX = mybir.AxisListType

S = 2048
D = 2048
NT = S // 128
ND = D // 128
NH = 16
I0 = 6144
NI0 = I0 // 128
IM = 1408
NIM = IM // 128
NE = 8
EPS = 1e-5
INV_SQRT_HD = float(1.0 / np.sqrt(128.0))
NEG = -1e30

UPTO = 4
DEBUG_TAPS = ()


def _alibi_slopes(n_heads):
    start = 2.0 ** (-8.0 / n_heads)
    return [start * start ** i for i in range(n_heads)]


class Ctr:
    def __init__(self, nc, ctx, name):
        self.h = ctx.enter_context(nc.semaphore(name))
        self.n = 0

    def inc(self, instr, k=1):
        instr.then_inc(self.h, k)
        self.n += k
        return instr

    def incd(self, instr):
        return self.inc(instr, 16)

    def wait(self, eng, at=None):
        v = self.n if at is None else at
        if v > 0:
            eng.wait_ge(self.h, v)


class PCtr:

    def __init__(self, nc, ctx, name, k=2):
        self.c = [Ctr(nc, ctx, f"{name}{i}") for i in range(k)]
        self.k = k

    def incd(self, instr, i):
        return self.c[i % self.k].incd(instr)

    def wait(self, eng, i, at=None):
        self.c[i % self.k].wait(eng, at)

    def wait_all(self, eng):
        for c in self.c:
            c.wait(eng)


def build_program(upto=None, taps=None, probe=None):
    upto = UPTO if upto is None else upto
    taps = DEBUG_TAPS if taps is None else taps
    nc = bass.Bass("TRN2", target_bir_lowering=False)

    dt_in = lambda nm, shp: nc.dram_tensor(nm, shp, F32, kind="ExternalInput").ap()
    x0_d = dt_in("x0", [S, D])
    wq_d, wk_d, wv_d, wo_d = (dt_in(n, [D, D]) for n in ["wq", "wk", "wv", "wo"])
    bq_d, bk_d, bv_d, bo_d = (dt_in(n, [1, D]) for n in ["bq", "bk", "bv", "bo"])
    gw_d, uw_d = dt_in("gw", [D, I0]), dt_in("uw", [D, I0])
    dw_d = dt_in("dw", [I0, D])
    gs_d, us_d = dt_in("gs", [128, NI0]), dt_in("us", [128, NI0])
    ds_d = dt_in("ds", [128, ND])
    w2_d = dt_in("w2", [16, 128, 384])
    bqkv2_d = dt_in("bqkv2", [1, 16 * 384])
    woutbd_d = dt_in("woutbd", [128, 128])
    bout2_d = dt_in("bout2", [1, 128])
    rw_d = dt_in("rw", [8, D])
    mgw_d = dt_in("mgw", [NE, D, IM])
    muw_d = dt_in("muw", [NE, D, IM])
    mdw_d = dt_in("mdw", [NE, IM, D])
    ident_d = nc.dram_tensor("ident", [128, 128], BF16, kind="ExternalInput").ap()
    iota2_d = dt_in("iota2", [2, S])
    init4_d = nc.dram_tensor("init4", [4, S], BF16, kind="ExternalInput").ap()
    mqk_d = dt_in("mqk", [128, 128])
    mkq_d = dt_in("mkq", [128, 128])
    onesb_d = nc.dram_tensor("onesb", [1, 512], BF16, kind="ExternalInput").ap()
    onesc_d = nc.dram_tensor("onesc", [128, 1], BF16, kind="ExternalInput").ap()

    out_d = nc.dram_tensor("out", [S, D], F32, kind="ExternalOutput").ap()

    q_s = nc.dram_tensor("q_s", [D, S], BF16, kind="Internal").ap()
    k_s = nc.dram_tensor("k_s", [D, S], BF16, kind="Internal").ap()
    v_s = nc.dram_tensor("v_s", [D, S], BF16, kind="Internal").ap()
    x1_s = nc.dram_tensor("x1_s", [S, D], F32, kind="Internal").ap()
    x2_s = nc.dram_tensor("x2_s", [S, D], F32, kind="Internal").ap()
    x3_s = nc.dram_tensor("x3_s", [S, D], F32, kind="Internal").ap()
    tapt = {"x1_s": x1_s, "x2_s": x2_s, "x3_s": x3_s}
    tap_out = {t: nc.dram_tensor("tap_" + t, [S, D], F32, kind="ExternalOutput").ap()
               for t in taps}

    ctx = ExitStack()
    sb = lambda name, shape, dt: ctx.enter_context(nc.sbuf_tensor(name, shape, dt))

    ident = sb("identb", [128, 128], BF16)
    onesb = sb("onesbb", [1, 512], BF16)
    onesc = sb("onescb", [128, 1], BF16)
    gs_sb = sb("gs_sb", [128, NI0], F32)
    us_sb = sb("us_sb", [128, NI0], F32)
    ds_sb = sb("ds_sb", [128, ND], F32)
    st6 = sb("st6", [128, 24], F32)
    mv2 = sb("mv2", [128, 2], F32)
    tmp1 = sb("tmp1", [128, 1], F32)
    istd = sb("istd", [128, 1], F32)
    nmi = sb("nmi", [128, 1], F32)

    block = ctx.enter_context(nc.Block())

    sem_names = ["ldc", "ldcg", "ldx", "ldw", "mm", "ev", "vv", "ss", "tt",
                 "st", "stg", "ldh", "ph", "fr", "acc", "aux", "lg"]
    C = {nm: Ctr(nc, ctx, nm) for nm in sem_names}
    LDC, LDCG, LDX, LDW, MM, EV, VV, SS, TT, ST, STG = (C[n] for n in
        ["ldc", "ldcg", "ldx", "ldw", "mm", "ev", "vv", "ss", "tt", "st", "stg"])
    LDH, PH, FR, ACC, AUX, LG = (C[n] for n in ["ldh", "ph", "fr", "acc", "aux", "lg"])
    LDXP = PCtr(nc, ctx, "ldxp", 2)
    LDWP = PCtr(nc, ctx, "ldwp", 2)
    LDWD = PCtr(nc, ctx, "ldwd", 3)
    LDWG = PCtr(nc, ctx, "ldwg", 2)
    LDWB = PCtr(nc, ctx, "ldwb", 2)
    LDWM = PCtr(nc, ctx, "ldwm", 2)
    STP = PCtr(nc, ctx, "stp", 4)

    @block.sync
    def _(sync):
        LDC.incd(sync.dma_start(out=ident[:], in_=ident_d))
        LDC.incd(sync.dma_start(out=onesb[:], in_=onesb_d))
        LDC.incd(sync.dma_start(out=onesc[:], in_=onesc_d))
        LDC.incd(sync.dma_start(out=gs_sb[:], in_=gs_d))
        LDC.incd(sync.dma_start(out=us_sb[:], in_=us_d))
        LDC.incd(sync.dma_start(out=ds_sb[:], in_=ds_d))

    def emit_ln_tile(x_ap, out_ap, pre_waits_v=(), pre_waits_s=()):
        @block.vector
        def _(vector):
            for cn, at in pre_waits_v:
                cn.wait(vector, at)
            SS.wait(vector)
            for j in range(4):
                vector.bn_stats(out=st6[:, 6 * j:6 * j + 6],
                                in_=x_ap[:, 512 * j:512 * (j + 1)])
            vector.drain()
            vector.bn_aggr(out=mv2[:], in_=st6[:])
            vector.drain()
            vector.tensor_scalar(out=tmp1[:], in0=mv2[:, 1:2], scalar1=EPS,
                                 scalar2=None, op0=OP.add)
            vector.drain()
            vector.drain()
            VV.inc(vector.engine_nop())

        @block.scalar
        def _(scalar):
            VV.wait(scalar)
            scalar.activation(out=tmp1[:], in_=tmp1[:], func=AF.Ln)
            scalar.drain()
            scalar.activation(out=istd[:], in_=tmp1[:], func=AF.Exp, scale=-0.5)
            scalar.drain()
            SS.inc(scalar.nop())

        @block.vector
        def _(vector):
            SS.wait(vector)
            vector.tensor_tensor(out=nmi[:], in0=mv2[:, 0:1], in1=istd[:],
                                 op=OP.mult)
            vector.drain()
            vector.tensor_scalar(out=nmi[:], in0=nmi[:], scalar1=-1.0,
                                 scalar2=None, op0=OP.mult)
            vector.drain()
            vector.drain()
            VV.inc(vector.engine_nop())

        @block.scalar
        def _(scalar):
            VV.wait(scalar)
            for cn, at in pre_waits_s:
                cn.wait(scalar, at)
            SS.inc(scalar.activation(out=out_ap, in_=x_ap, func=AF.Identity,
                                     scale=istd[:, :], bias=nmi[:, :]))

    def emit_transpose_chunks(src_ap, n, dst_fn, ps_tp, pre=()):
        for g in range((n + 3) // 4):
            cnt = min(4, n - 4 * g)

            @block.tensor
            def _(tensor, g=g, cnt=cnt):
                for cn, at in pre:
                    cn.wait(tensor, at)
                FR.wait(tensor)
                for u in range(cnt):
                    j = 4 * g + u
                    ins = tensor.transpose(ps_tp[:, 128 * u:128 * (u + 1)],
                                           src_ap[:, 128 * j:128 * (j + 1)],
                                           ident[:])
                    if u == cnt - 1:
                        MM.inc(ins)

            @block.vector
            def _(vector, g=g, cnt=cnt):
                MM.wait(vector)
                for u in range(cnt):
                    j = 4 * g + u
                    vector.tensor_copy(out=dst_fn(j),
                                       in_=ps_tp[:, 128 * u:128 * (u + 1)])
                vector.drain()
                FR.inc(vector.engine_nop())

    phA = ExitStack()
    sbA = lambda nm, shp, dt: phA.enter_context(nc.sbuf_tensor(nm, shp, dt))
    iota2 = sbA("iota2b", [2, S], F32)
    mqk = sbA("mqkb", [128, 128], F32)
    mkq = sbA("mkqb", [128, 128], F32)
    brow = {nm: sbA(nm + "row", [1, D], BF16) for nm in ["bq", "bk", "bv", "bo"]}

    @block.sync
    def _(sync):
        LDC.incd(sync.dma_start(out=iota2[:], in_=iota2_d))
        LDC.incd(sync.dma_start(out=mqk[:], in_=mqk_d))
        LDC.incd(sync.dma_start(out=mkq[:], in_=mkq_d))

    @block.gpsimd
    def _(gpsimd):
        for nm, dd in [("bq", bq_d), ("bk", bk_d), ("bv", bv_d), ("bo", bo_d)]:
            LDCG.incd(gpsimd.dma_start(out=brow[nm][:], in_=dd))
    brow_at = LDCG.n
    ldc_at = LDC.n

    phA1 = ExitStack()
    xt = [phA1.enter_context(nc.sbuf_tensor(f"xt{i}", [128, D], F32))
          for i in range(2)]
    xnst = phA1.enter_context(nc.sbuf_tensor("xnst", [128, D], BF16))
    xnT = phA1.enter_context(nc.sbuf_tensor("xnT", [128, ND * S], BF16))
    ps_tpA = phA1.enter_context(nc.psum_tensor("ps_tpA", [128, 512], BF16))

    xn_done_at = []
    tp_done_at = []
    for t in range(NT):
        @block.sync
        def _(sync, t=t):
            if t >= 2:
                SS.wait(sync, xn_done_at[t - 2])
            LDXP.incd(sync.dma_start(out=xt[t % 2][:],
                                     in_=x0_d[128 * t:128 * (t + 1), :]), t)

        pw_s = [(MM, MM.n)] if t > 0 else []
        emit_ln_tile(xt[t % 2][:], xnst[:],
                     pre_waits_v=[(LDXP.c[t % 2], LDXP.c[t % 2].n),
                                  (LDC, LDC.n)], pre_waits_s=pw_s)
        xn_done_at.append(SS.n)
        emit_transpose_chunks(
            xnst[:], ND,
            lambda j, t=t: xnT[:, j * S + 128 * t: j * S + 128 * t + 128],
            ps_tpA, pre=[(SS, SS.n)])
        tp_done_at.append(FR.n)

    xnT_at = FR.n

    if probe == "xnT":
        pout = nc.dram_tensor("probe0", [128, ND * S], F32,
                              kind="ExternalOutput").ap()

        @block.gpsimd
        def _(gpsimd):
            FR.wait(gpsimd, xnT_at)
            ACC.incd(gpsimd.dma_start(out=pout, in_=xnT[:]))

        @block.sync
        def _(sync):
            ACC.wait(sync)
        ctx.pop_all()
        return nc

    wbuf = [phA1.enter_context(nc.sbuf_tensor(f"wbufA{i}", [128, ND * 128], BF16))
            for i in range(2)]
    stgA = [phA1.enter_context(nc.sbuf_tensor(f"stgA{i}", [128, 512], BF16))
            for i in range(4)]
    ps_p = [phA1.enter_context(nc.psum_tensor(f"ps_p{i}", [128, 512], F32))
            for i in range(2)]

    slab_used = []

    def qkv_pass(w_dram, b_ap, dst, scale):
        for do in range(ND):
            @block.gpsimd
            def _(gpsimd, do=do):
                if len(slab_used) >= 2:
                    MM.wait(gpsimd, slab_used[-2])
                wv_ = w_dram.rearrange("(dc dp) o -> dp dc o", dp=128)
                LDWP.incd(gpsimd.dma_start(
                    out=wbuf[do % 2][:].rearrange("p (dc o) -> p dc o", dc=ND),
                    in_=wv_[:, :, 128 * do:128 * (do + 1)]), do)

            for tb in range(4):
                @block.tensor
                def _(tensor, do=do, tb=tb):
                    LDWP.wait(tensor, do)
                    LDC.wait(tensor, ldc_at)
                    LDCG.wait(tensor, brow_at)
                    FR.wait(tensor, xnT_at)
                    if EV.n >= 2:
                        EV.wait(tensor, EV.n - 1)
                    ps = ps_p[(4 * do + tb) % 2]
                    for d in range(ND):
                        tensor.matmul(ps[:],
                                      wbuf[do % 2][:, 128 * d:128 * (d + 1)],
                                      xnT[:, d * S + 512 * tb:d * S + 512 * (tb + 1)],
                                      start=(d == 0), stop=False)
                    MM.inc(tensor.matmul(
                        ps[:], b_ap[0:1, 128 * do:128 * (do + 1)],
                        onesb[0:1, 0:512], start=False, stop=True))

                @block.scalar
                def _(scalar, do=do, tb=tb):
                    MM.wait(scalar)
                    STP.wait(scalar, 4 * do + tb)
                    EV.inc(scalar.activation(out=stgA[(4 * do + tb) % 4][:],
                                             in_=ps_p[(4 * do + tb) % 2][:],
                                             func=AF.Copy, scale=float(scale)))

                @block.sync
                def _(sync, do=do, tb=tb):
                    EV.wait(sync)
                    STP.incd(sync.dma_start(
                        out=dst[128 * do:128 * (do + 1), 512 * tb:512 * (tb + 1)],
                        in_=stgA[(4 * do + tb) % 4][:]), 4 * do + tb)
            slab_used.append(MM.n)

    qkv_pass(wq_d, brow["bq"], q_s, INV_SQRT_HD)
    qkv_pass(wk_d, brow["bk"], k_s, 1.0)
    qkv_pass(wv_d, brow["bv"], v_s, 1.0)
    a2_st_at = [STP.c[i].n for i in range(4)]

    if probe == "qkv":
        pq = nc.dram_tensor("probe_q", [D, S], F32, kind="ExternalOutput").ap()
        pk = nc.dram_tensor("probe_k", [D, S], F32, kind="ExternalOutput").ap()
        pv = nc.dram_tensor("probe_v", [D, S], F32, kind="ExternalOutput").ap()

        @block.gpsimd
        def _(gpsimd):
            for _i in range(4):
                STP.c[_i].wait(gpsimd, a2_st_at[_i])
            ACC.incd(gpsimd.dma_start(out=pq, in_=q_s))
            ACC.incd(gpsimd.dma_start(out=pk, in_=k_s))
            ACC.incd(gpsimd.dma_start(out=pv, in_=v_s))

        @block.sync
        def _(sync):
            ACC.wait(sync)
        ctx.pop_all()
        return nc
    phA1.close()

    st_att = ExitStack()
    att = [st_att.enter_context(nc.sbuf_tensor(f"att{t}", [128, D], BF16,
                                               side="right"))
           for t in range(NT)]
    phA3 = ExitStack()
    sb3 = lambda nm, shp, dt: phA3.enter_context(nc.sbuf_tensor(nm, shp, dt))
    qh = [sb3(f"qh{i}", [128, S], BF16) for i in range(2)]
    kh = [sb3(f"kh{i}", [128, S], BF16) for i in range(2)]
    vhf = [sb3(f"vhf{i}", [128, S], BF16) for i in range(2)]
    vtok = sb3("vtok", [128, S], BF16)
    tileA = [sb3(f"tileA{i}", [4, S], BF16) for i in range(2)]
    tileB = [sb3(f"tileB{i}", [4, S], BF16) for i in range(2)]
    alo_r = sb3("alo_r", [1, S], BF16)
    nbhi_r = sb3("nbhi_r", [1, S], BF16)
    nblo_r = sb3("nblo_r", [1, S], BF16)
    arow = sb3("arow", [1, S], F32)
    sqbuf = sb3("sqbuf", [128, S], BF16)
    krow = sb3("krow", [1, S], F32)
    qrow = sb3("qrow", [1, S], F32)
    kmx = sb3("kmx", [1, 1], F32)
    bnd = sb3("bnd", [1, S], F32)
    nmfull = sb3("nmfull", [1, S], BF16)
    rinv = sb3("rinv", [128, 1], F32)
    PT = sb3("PT", [128, 16 * 512], BF16)
    ps_tpB = phA3.enter_context(nc.psum_tensor("ps_tpB", [128, 512], BF16))
    ps_sT = [phA3.enter_context(nc.psum_tensor(f"ps_sT{i}", [128, 512], F32))
             for i in range(2)]
    ps_av = [phA3.enter_context(nc.psum_tensor(f"ps_av{i}", [128, 512], F32))
             for i in range(2)]
    ps_r = [phA3.enter_context(nc.psum_tensor(f"ps_r{i}", [128, 16], F32))
            for i in range(2)]
    ps_bd = phA3.enter_context(nc.psum_tensor("ps_bd", [1, 512], F32))

    LDHP = PCtr(nc, ctx, "ldhp", 2)
    head_done_at = []
    head_mm_at = []
    aux_at = []
    avev = []
    prev_av_at = 0
    slopes_all = _alibi_slopes(NH)
    nh_build = 1 if probe == "att1" else NH
    for h in range(nh_build):
        @block.sync
        def _(sync, h=h):
            if h == 0:
                for _i in range(4):
                    STP.c[_i].wait(sync, a2_st_at[_i])
                LDC.wait(sync)
                LDHP.incd(sync.dma_start(out=tileA[0][:], in_=init4_d), 0)
                LDHP.incd(sync.dma_start(out=tileA[1][:], in_=init4_d), 1)
                LDHP.incd(sync.dma_start(out=tileB[0][:], in_=init4_d), 0)
                LDHP.incd(sync.dma_start(out=tileB[1][:], in_=init4_d), 1)
            if h >= 2:
                MM.wait(sync, head_mm_at[h - 2])
            LDHP.incd(sync.dma_start(out=qh[h % 2][:],
                                     in_=q_s[128 * h:128 * (h + 1), :]), h)
            LDHP.incd(sync.dma_start(out=kh[h % 2][:],
                                     in_=k_s[128 * h:128 * (h + 1), :]), h)
            LDHP.incd(sync.dma_start(out=vhf[h % 2][:],
                                     in_=v_s[128 * h:128 * (h + 1), :]), h)

        slope = float(slopes_all[h])

        @block.scalar
        def _(scalar, h=h, slope=slope):
            LDHP.wait(scalar, h)
            if len(head_mm_at) >= 2:
                MM.wait(scalar, head_mm_at[h - 2])
            if aux_at:
                AUX.wait(scalar, aux_at[-1])
            scalar.activation(out=arow[:], in_=iota2[0:1, :], func=AF.Copy,
                              scale=slope)
            scalar.drain()
            scalar.activation(out=tileB[h % 2][0:1, :], in_=arow[:], func=AF.Copy)
            scalar.drain()
            SS.inc(scalar.nop())

        @block.vector
        def _(vector, h=h):
            SS.wait(vector)
            vector.tensor_tensor(out=alo_r[:], in0=arow[:],
                                 in1=tileB[h % 2][0:1, :], op=OP.subtract)
            vector.drain()
            VV.inc(vector.engine_nop())

        @block.scalar
        def _(scalar, h=h, slope=slope):
            VV.wait(scalar)
            scalar.activation(out=arow[:], in_=iota2[0:1, :], func=AF.Copy,
                              scale=-slope)
            scalar.drain()
            scalar.activation(out=nbhi_r[:], in_=arow[:], func=AF.Copy)
            scalar.drain()
            SS.inc(scalar.nop())

        @block.vector
        def _(vector, h=h):
            SS.wait(vector)
            vector.tensor_tensor(out=nblo_r[:], in0=arow[:],
                                 in1=nbhi_r[:], op=OP.subtract)
            vector.drain()
            VV.inc(vector.engine_nop())

        @block.gpsimd
        def _(gpsimd, h=h):
            VV.wait(gpsimd)
            if len(head_mm_at) >= 2:
                MM.wait(gpsimd, head_mm_at[h - 2])
            AUX.incd(gpsimd.dma_start(out=tileB[h % 2][1:2, :], in_=alo_r[:]))
            AUX.incd(gpsimd.dma_start(out=tileA[h % 2][2:3, :], in_=nbhi_r[:]))
            AUX.incd(gpsimd.dma_start(out=tileA[h % 2][3:4, :], in_=nblo_r[:]))
        aux_at.append(AUX.n)
        tiles_at = AUX.n

        emit_transpose_chunks(vhf[h % 2][:], NT,
                              lambda j: vtok[:, 128 * j:128 * (j + 1)],
                              ps_tpB, pre=[(LDHP.c[h % 2], LDHP.c[h % 2].n)])
        vtok_at = FR.n

        @block.vector
        def _(vector, h=h):
            LDHP.wait(vector, h)
            SS.wait(vector)
            vector.tensor_tensor(out=sqbuf[:], in0=kh[h % 2][:],
                                 in1=kh[h % 2][:], op=OP.mult)
            vector.drain()
            VV.inc(vector.engine_nop())
        for j in range(4):
            @block.tensor
            def _(tensor, j=j):
                VV.wait(tensor)
                SS.wait(tensor)
                MM.inc(tensor.matmul(ps_bd[:, :], onesc[:, :],
                                     sqbuf[:, 512 * j:512 * (j + 1)],
                                     start=True, stop=True))

            @block.scalar
            def _(scalar, j=j):
                MM.wait(scalar)
                SS.inc(scalar.activation(out=krow[0:1, 512 * j:512 * (j + 1)],
                                         in_=ps_bd[:, :], func=AF.Copy))

        @block.vector
        def _(vector, h=h):
            SS.wait(vector)
            vector.tensor_tensor(out=sqbuf[:], in0=qh[h % 2][:],
                                 in1=qh[h % 2][:], op=OP.mult)
            vector.drain()
            vector.tensor_reduce(out=kmx[:], in_=krow[:], axis=AX.X, op=OP.max)
            vector.drain()
            VV.inc(vector.engine_nop())
        for j in range(4):
            @block.tensor
            def _(tensor, j=j):
                VV.wait(tensor)
                SS.wait(tensor)
                MM.inc(tensor.matmul(ps_bd[:, :], onesc[:, :],
                                     sqbuf[:, 512 * j:512 * (j + 1)],
                                     start=True, stop=True))

            @block.scalar
            def _(scalar, j=j):
                MM.wait(scalar)
                SS.inc(scalar.activation(out=qrow[0:1, 512 * j:512 * (j + 1)],
                                         in_=ps_bd[:, :], func=AF.Copy))

        @block.vector
        def _(vector, h=h):
            SS.wait(vector)
            vector.tensor_scalar(out=bnd[:], in0=qrow[:], scalar1=kmx[0:1, 0:1],
                                 scalar2=1e-30, op0=OP.mult, op1=OP.add)
            vector.drain()
            VV.inc(vector.engine_nop())

        @block.scalar
        def _(scalar, h=h):
            VV.wait(scalar)
            scalar.activation(out=bnd[:], in_=bnd[:], func=AF.Ln)
            scalar.drain()
            scalar.activation(out=bnd[:], in_=bnd[:], func=AF.Exp, scale=0.5)
            scalar.drain()
            SS.inc(scalar.nop())

        @block.vector
        def _(vector, h=h):
            SS.wait(vector)
            vector.tensor_scalar(out=nmfull[:], in0=bnd[:], scalar1=-1.0,
                                 scalar2=None, op0=OP.mult)
            vector.drain()
            VV.inc(vector.engine_nop())
        nm_at = VV.n

        exp_ss = []
        for qtg in range(4):
            for kt in range(4 * (qtg + 1)):
                @block.tensor
                def _(tensor, h=h, qtg=qtg, kt=kt):
                    VV.wait(tensor, nm_at)
                    AUX.wait(tensor, tiles_at)
                    if len(exp_ss) >= 2:
                        SS.wait(tensor, exp_ss[-2])
                    ps = ps_sT[kt % 2]
                    tensor.matmul(ps[:], kh[h % 2][:, 128 * kt:128 * (kt + 1)],
                                  qh[h % 2][:, 512 * qtg:512 * (qtg + 1)],
                                  start=True, stop=False)
                    tensor.matmul(ps[:], tileB[h % 2][:, 128 * kt:128 * (kt + 1)],
                                  tileA[h % 2][:, 512 * qtg:512 * (qtg + 1)],
                                  start=False, stop=False)
                    MM.inc(tensor.matmul(ps[:], onesb[0:1, 0:128],
                                         nmfull[0:1, 512 * qtg:512 * (qtg + 1)],
                                         start=False, stop=True))

                u = kt - 4 * qtg
                if 0 <= u < 4:
                    @block.vector
                    def _(vector, qtg=qtg, kt=kt, u=u):
                        MM.wait(vector)
                        MM.wait(vector, prev_av_at)
                        vector.tensor_tensor(
                            out=ps_sT[kt % 2][:, 128 * u:128 * (u + 1)],
                            in0=ps_sT[kt % 2][:, 128 * u:128 * (u + 1)],
                            in1=mkq[:], op=OP.add)
                        if u > 0:
                            vector.memset(PT[:, 512 * kt:512 * kt + 128 * u], 0.0)
                        vector.drain()
                        VV.inc(vector.engine_nop())

                @block.scalar
                def _(scalar, qtg=qtg, kt=kt, u=u):
                    if 0 <= u < 4:
                        VV.wait(scalar)
                    else:
                        MM.wait(scalar)
                    MM.wait(scalar, prev_av_at)
                    lo = 128 * u if 0 < u < 4 else 0
                    SS.inc(scalar.activation(
                        out=PT[:, 512 * kt + lo:512 * (kt + 1)],
                        in_=ps_sT[kt % 2][:, lo:512], func=AF.Exp))
                exp_ss.append(SS.n)
            exp_at = SS.n

            for u in range(4):
                qt = 4 * qtg + u

                @block.tensor
                def _(tensor, h=h, u=u, qt=qt):
                    SS.wait(tensor, exp_at)
                    LDC.wait(tensor, ldc_at)
                    FR.wait(tensor, vtok_at)
                    if len(avev) >= 2:
                        EV.wait(tensor, avev[-2])
                    pa, pr = ps_av[u % 2], ps_r[u % 2]
                    for kt in range(qt + 1):
                        tensor.matmul(pa[:, 0:128],
                                      PT[:, 512 * kt + 128 * u:512 * kt + 128 * (u + 1)],
                                      vtok[:, 128 * kt:128 * (kt + 1)],
                                      start=(kt == 0), stop=(kt == qt),
                                      skip_group_check=True)
                        ins = tensor.matmul(pr[:, 0:1],
                                            PT[:, 512 * kt + 128 * u:512 * kt + 128 * (u + 1)],
                                            onesc[:, :], start=(kt == 0),
                                            stop=(kt == qt),
                                            skip_group_check=True)
                    MM.inc(ins)

                @block.vector
                def _(vector, h=h, u=u, qt=qt):
                    MM.wait(vector)
                    vector.reciprocal(out=rinv[:], in_=ps_r[u % 2][:, 0:1])
                    vector.drain()
                    vector.tensor_scalar(out=att[qt][:, 128 * h:128 * (h + 1)],
                                         in0=ps_av[u % 2][:, 0:128],
                                         scalar1=rinv[:, :],
                                         scalar2=None, op0=OP.mult)
                    vector.drain()
                    EV.inc(vector.engine_nop())
                avev.append(EV.n)
            prev_av_at = MM.n
        head_mm_at.append(MM.n)

        @block.vector
        def _(vector):
            EV.wait(vector)
            vector.drain()
            PH.inc(vector.engine_nop())
        head_done_at.append(PH.n)

    phA3.close()

    phA4 = ExitStack()
    attT = phA4.enter_context(nc.sbuf_tensor("attT", [128, ND * S], BF16))
    ps_tpC = phA4.enter_context(nc.psum_tensor("ps_tpC", [128, 512], BF16))
    for t in range(NT):
        emit_transpose_chunks(
            att[t][:], ND,
            lambda j, t=t: attT[:, j * S + 128 * t:j * S + 128 * t + 128],
            ps_tpC, pre=[(PH, head_done_at[-1])])
    attT_at = FR.n
    st_att.close()

    x0t = [phA4.enter_context(nc.sbuf_tensor(f"x0t{u}", [128, D], F32))
           for u in range(4)]
    wob = [phA4.enter_context(nc.sbuf_tensor(f"wob{i}", [128, ND * 128], BF16))
           for i in range(2)]
    stgo = [phA4.enter_context(nc.sbuf_tensor(f"stgo{i}", [128, 512], BF16))
            for i in range(2)]
    xpc = [phA4.enter_context(nc.sbuf_tensor(f"xpc{u}", [128, 128], F32))
           for u in range(4)]
    ps_o = [phA4.enter_context(nc.psum_tensor(f"ps_o{i}", [128, 512], F32))
            for i in range(2)]
    ps_ot = phA4.enter_context(nc.psum_tensor("ps_ot", [128, 512], BF16))

    slab_used = []
    res_done_at = []
    it = 0
    for tb in range(4):
        @block.sync
        def _(sync, tb=tb):
            if tb > 0:
                VV.wait(sync, res_done_at[-1])
            else:
                PH.wait(sync, head_done_at[-1])
            for u in range(4):
                t = 4 * tb + u
                LDX.incd(sync.dma_start(out=x0t[u][:],
                                        in_=x0_d[128 * t:128 * (t + 1), :]))

        for do in range(ND):
            @block.gpsimd
            def _(gpsimd, tb=tb, do=do, it=it):
                if len(slab_used) >= 2:
                    MM.wait(gpsimd, slab_used[-2])
                else:
                    PH.wait(gpsimd, head_done_at[-1])
                wv_ = wo_d.rearrange("(dc dp) o -> dp dc o", dp=128)
                LDWP.incd(gpsimd.dma_start(
                    out=wob[it % 2][:].rearrange("p (dc o) -> p dc o", dc=ND),
                    in_=wv_[:, :, 128 * do:128 * (do + 1)]), it)

            @block.tensor
            def _(tensor, tb=tb, do=do, it=it):
                LDWP.wait(tensor, it)
                LDC.wait(tensor, ldc_at)
                LDCG.wait(tensor, brow_at)
                FR.wait(tensor, attT_at)
                if it == 0:
                    EV.wait(tensor)
                elif EV.n >= 1:
                    EV.wait(tensor, EV.n - 1)
                ps = ps_o[it % 2]
                for d in range(ND):
                    tensor.matmul(ps[:], wob[it % 2][:, 128 * d:128 * (d + 1)],
                                  attT[:, d * S + 512 * tb:d * S + 512 * (tb + 1)],
                                  start=(d == 0), stop=False)
                MM.inc(tensor.matmul(ps[:],
                                     brow["bo"][0:1, 128 * do:128 * (do + 1)],
                                     onesb[0:1, 0:512], start=False, stop=True))
            slab_used.append(MM.n)

            @block.scalar
            def _(scalar, tb=tb, do=do):
                MM.wait(scalar)
                TT.wait(scalar)
                EV.inc(scalar.activation(out=stgo[it % 2][:], in_=ps_o[it % 2][:],
                                         func=AF.Copy))

            @block.tensor
            def _(tensor, tb=tb, do=do):
                EV.wait(tensor)
                VV.wait(tensor)
                for u in range(4):
                    ins = tensor.transpose(ps_ot[:, 128 * u:128 * (u + 1)],
                                           stgo[it % 2][:, 128 * u:128 * (u + 1)],
                                           ident[:])
                TT.inc(ins)

            @block.vector
            def _(vector, tb=tb, do=do):
                TT.wait(vector)
                LDX.wait(vector)
                STG.wait(vector)
                for u in range(4):
                    vector.scalar_tensor_tensor(
                        out=xpc[u][:], in0=ps_ot[:, 128 * u:128 * (u + 1)],
                        scalar=1.0, in1=x0t[u][:, 128 * do:128 * (do + 1)],
                        op0=OP.mult, op1=OP.add)
                vector.drain()
                VV.inc(vector.engine_nop())

            @block.sync
            def _(sync, tb=tb, do=do):
                VV.wait(sync)
                for u in range(4):
                    t = 4 * tb + u
                    STG.incd(sync.dma_start(
                        out=x1_s[128 * t:128 * (t + 1), 128 * do:128 * (do + 1)],
                        in_=xpc[u][:]))
            it += 1
        res_done_at.append(VV.n)
    phA4.close()
    phA.close()

    if upto >= 2:
        phBC = ExitStack()
        sbB = lambda nm, shp, dt: phBC.enter_context(nc.sbuf_tensor(nm, shp, dt))
        xn2T = [sbB(f"xn2T{i}", [128, ND * 512], BF16) for i in range(2)]
        fuse = sbB("fuse", [128, NI0 * 512], BF16)
        g8 = [sbB(f"g8{i}", [128, ND * 128], F8) for i in range(2)]
        u8 = [sbB(f"u8{i}", [128, ND * 128], F8) for i in range(2)]
        d8 = [sbB(f"d8{i}", [128, NI0 * 128], F8) for i in range(2)]
        gact = sbB("gact", [128, 512], BF16)
        x1t = [sbB("x1t0", [128, D], F32)]
        x1t.append(x1t[0])
        xn2st = sbB("xn2st", [128, D], BF16)
        x1rs = [sbB(f"x1rs{u}", [128, 128], F32) for u in range(4)]
        stgd = [sbB(f"stgd{i}", [128, 512], BF16) for i in range(2)]
        xpcF = [sbB(f"xpcF{u}", [128, 128], F32) for u in range(4)]
        if upto >= 3:
            w2b = sbB("w2b", [128, 16 * 384], BF16)
            bqkv2r = sbB("bqkv2r", [1, 16 * 384], BF16)
            wobd = sbB("wobd", [128, 128], BF16)
            bo2r = sbB("bo2r", [1, 128], BF16)
            x2t = [sbB("x2t0", [128, D], F32)]
            x2t.append(x2t[0])
            xn3 = sbB("xn3", [128, D], BF16)
            xn3T = xn3
            qkv = sbB("qkv", [128, 3 * D], BF16)
            sct = [sbB(f"sct{i}", [128, 2048], F32) for i in range(2)]
            p0 = [sbB(f"p0{i}", [128, 2048], BF16) for i in range(2)]
            zz = sbB("zz", [128, 64], F32)
            rz = sbB("rz", [128, 64], F32)
            v2 = sbB("v2", [128, 64], BF16)
            ctxa = sbB("ctxa", [128, D], BF16)
            ctxT = sbB("ctxT", [128, D], BF16)
            x3p = [sbB("x3p0", [128, 512], F32)]
            x3p.append(x3p[0])
        ps_tpF = phBC.enter_context(nc.psum_tensor("ps_tpF", [128, 512], BF16))
        ps_gB = [phBC.enter_context(nc.psum_tensor(f"ps_gB{i}", [128, 512], F32))
                 for i in range(2)]
        ps_uB = [phBC.enter_context(nc.psum_tensor(f"ps_uB{i}", [128, 512], F32))
                 for i in range(2)]
        ps_sh = [phBC.enter_context(nc.psum_tensor(f"ps_sh{i}", [128, 512], F32))
                 for i in range(2)]
        ps_otF = phBC.enter_context(nc.psum_tensor("ps_otF", [128, 512], BF16))
        sh_ev = [Ctr(nc, ctx, f"shev{i}") for i in range(2)]
        LDXC = PCtr(nc, ctx, "ldxc", 2)
        STC = Ctr(nc, ctx, "stc")

        if upto >= 3:
            @block.gpsimd
            def _(gpsimd):
                MM.wait(gpsimd)
                VV.wait(gpsimd)
                SS.wait(gpsimd)
                STG.wait(gpsimd)
                LDCG.incd(gpsimd.dma_start(
                    out=w2b[:].rearrange("p (c o) -> p c o", c=16),
                    in_=w2_d.transpose([1, 0, 2])))
                LDCG.incd(gpsimd.dma_start(out=bqkv2r[:], in_=bqkv2_d))
                LDCG.incd(gpsimd.dma_start(out=wobd[:], in_=woutbd_d))
                LDCG.incd(gpsimd.dma_start(out=bo2r[:], in_=bout2_d))
            w2_at = LDCG.n

        b_x2_at = []
        c_state = {"resid_at": [], "lt": -1}

        def c_chunk_gen():
            if upto < 3:
                return
            for lt in range(NT):
                yield ("x2", lt // 4)
                @block.sync
                def _(sync, lt=lt):
                    STG.wait(sync, b_x2_at[lt // 4])
                    if c_state["resid_at"]:
                        VV.wait(sync, c_state["resid_at"][-1])
                    LDXC.incd(sync.dma_start(
                        out=x2t[lt % 2][:],
                        in_=x2_s[128 * lt:128 * (lt + 1), :]), lt)

                emit_ln_tile(x2t[lt % 2][:], xn3[:],
                             pre_waits_v=[(LDXC.c[lt % 2], LDXC.c[lt % 2].n)],
                             pre_waits_s=[(MM, MM.n)] if lt > 0 else [])
                yield None
                emit_transpose_chunks(xn3[:], ND,
                                      lambda j: xn3T[:, 128 * j:128 * (j + 1)],
                                      ps_tpF, pre=[(SS, SS.n)])
                xn3T_at = FR.n
                yield None
                for pr in range(16):
                    bk = pr % 2

                    @block.tensor
                    def _(tensor, lt=lt, pr=pr, bk=bk, xn3T_at=xn3T_at):
                        FR.wait(tensor, xn3T_at)
                        LDCG.wait(tensor, w2_at)
                        LDC.wait(tensor, ldc_at)
                        sh_ev[bk].wait(tensor)
                        ps = ps_sh[bk]
                        tensor.matmul(ps[:, 0:384],
                                      xn3T[:, 128 * pr:128 * (pr + 1)],
                                      w2b[:, 384 * pr:384 * (pr + 1)],
                                      start=True, stop=False)
                        MM.inc(tensor.matmul(ps[:, 0:384], onesb[0:1, 0:128],
                                             bqkv2r[0:1, 384 * pr:384 * (pr + 1)],
                                             start=False, stop=True))

                    @block.vector
                    def _(vector, pr=pr, bk=bk):
                        MM.wait(vector)
                        vector.tensor_copy(out=qkv[:, 384 * pr:384 * (pr + 1)],
                                           in_=ps_sh[bk][:, 0:384])
                        vector.drain()
                        sh_ev[bk].inc(vector.engine_nop())
                    if pr % 4 == 3:
                        yield None
                qkv_at = [sh_ev[0].n, sh_ev[1].n]

                qv_ = qkv[:].rearrange("p (c t d) -> p c t d", c=32, t=3)
                mult_vv = {}
                expd_ss = {}
                rest_vv = {-2: 0, -1: 0}

                def emit_mult(dcg, lt=lt, qkv_at=qkv_at):
                    @block.vector
                    def _(vector, dcg=dcg):
                        sh_ev[0].wait(vector, qkv_at[0])
                        sh_ev[1].wait(vector, qkv_at[1])
                        VV.wait(vector, rest_vv[dcg - 2])
                        qs = qv_[:, :, 0, 2 * dcg:2 * dcg + 2]
                        ks = qv_[:, :, 1, 2 * dcg:2 * dcg + 2]
                        qb = qs.unsqueeze(2).broadcast_to([128, 32, 32, 2])
                        kb = ks.unsqueeze(1).broadcast_to([128, 32, 32, 2])
                        sv = sct[dcg % 2][:].rearrange("p (c e d) -> p c e d",
                                                       c=32, e=32)
                        vector.tensor_tensor(out=sv, in0=qb, in1=kb, op=OP.mult)
                        vector.drain()
                        VV.inc(vector.engine_nop())
                    mult_vv[dcg] = VV.n

                def emit_exp(dcg):
                    @block.scalar
                    def _(scalar, dcg=dcg):
                        VV.wait(scalar, mult_vv[dcg])
                        VV.wait(scalar, rest_vv[dcg - 2])
                        SS.inc(scalar.activation(out=p0[dcg % 2][:],
                                                 in_=sct[dcg % 2][:],
                                                 func=AF.Exp, scale=0.125))
                    expd_ss[dcg] = SS.n

                def emit_rest(dcg, lt=lt):
                    @block.vector
                    def _(vector, dcg=dcg):
                        SS.wait(vector, expd_ss[dcg])
                        p0v = p0[dcg % 2][:].rearrange("p (c e d) -> p c e d",
                                                       c=32, e=32)
                        zv = zz[:].rearrange("p (c d) -> p c d", c=32)
                        vector.tensor_reduce(out=zv,
                                             in_=p0v.transpose([0, 1, 3, 2]),
                                             axis=AX.X, op=OP.add)
                        vector.drain()
                        rzv = rz[:].rearrange("p (c d) -> p c d", c=32)
                        vector.reciprocal(out=rzv, in_=zv)
                        vector.drain()
                        vslice = qv_[:, :, 2, 2 * dcg:2 * dcg + 2]
                        v2v = v2[:].rearrange("p (c d) -> p c d", c=32)
                        vector.tensor_tensor(out=v2v, in0=vslice, in1=rzv,
                                             op=OP.mult)
                        vector.drain()
                        v2b = v2[:].rearrange("p (c d) -> p c d", c=32)
                        v2b = v2b.unsqueeze(2).broadcast_to([128, 32, 32, 2])
                        prv = sct[dcg % 2][:].bitcast(BF16)[:, 0:2048].rearrange(
                            "p (c e d) -> p c e d", c=32, e=32)
                        vector.tensor_tensor(out=prv, in0=p0v, in1=v2b,
                                             op=OP.mult)
                        vector.drain()
                        cvw = ctxa[:].rearrange("p (m d) -> p m d", m=32)[:, :, 2 * dcg:2 * dcg + 2]
                        with nc.allow_low_precision(reason="fp32-accumulated reduce to bf16 matmul operand"):
                            vector.tensor_reduce(out=cvw,
                                                 in_=prv.transpose([0, 2, 3, 1]),
                                                 axis=AX.X, op=OP.add)
                        vector.drain()
                        VV.inc(vector.engine_nop())
                    rest_vv[dcg] = VV.n

                for dcg in range(32):
                    emit_mult(dcg)
                    emit_exp(dcg)
                    if dcg >= 1:
                        emit_rest(dcg - 1)
                    yield None
                emit_rest(31)

                emit_transpose_chunks(ctxa[:], ND,
                                      lambda j: ctxT[:, 128 * j:128 * (j + 1)],
                                      ps_tpF, pre=[(VV, VV.n)])
                ctxT_at = FR.n
                yield None
                for bkr in range(4):
                    bk = bkr % 2

                    @block.tensor
                    def _(tensor, lt=lt, bkr=bkr, bk=bk, ctxT_at=ctxT_at):
                        FR.wait(tensor, ctxT_at)
                        sh_ev[bk].wait(tensor)
                        for pr in range(4):
                            j = 4 * bkr + pr
                            tensor.matmul(ps_sh[bk][:, 128 * pr:128 * (pr + 1)],
                                          ctxT[:, 128 * j:128 * (j + 1)], wobd[:],
                                          start=True, stop=False,
                                          skip_group_check=True)
                            ins = tensor.matmul(
                                ps_sh[bk][:, 128 * pr:128 * (pr + 1)],
                                onesb[0:1, 0:128], bo2r[0:1, :],
                                start=False, stop=True, skip_group_check=True)
                        MM.inc(ins)

                    @block.vector
                    def _(vector, lt=lt, bkr=bkr, bk=bk):
                        MM.wait(vector)
                        STC.wait(vector)
                        vector.scalar_tensor_tensor(
                            out=x3p[bkr % 2][:], in0=ps_sh[bk][:],
                            scalar=1.0,
                            in1=x2t[lt % 2][:, 512 * bkr:512 * (bkr + 1)],
                            op0=OP.mult, op1=OP.add)
                        vector.drain()
                        sh_ev[bk].inc(vector.engine_nop())
                        VV.inc(vector.engine_nop())

                    @block.sync
                    def _(sync, lt=lt, bkr=bkr):
                        VV.wait(sync)
                        STC.incd(sync.dma_start(
                            out=x3_s[128 * lt:128 * (lt + 1),
                                     512 * bkr:512 * (bkr + 1)],
                            in_=x3p[bkr % 2][:]))
                c_state["resid_at"].append(VV.n)
                yield None

        cgen = c_chunk_gen()
        cgen_pending = [next(cgen, "DONE")] if upto >= 3 else ["DONE"]

        def drive_c(k=1):
            for _ in range(k):
                p = cgen_pending[0]
                if p == "DONE":
                    return
                if isinstance(p, tuple) and p[0] == "x2":
                    if len(b_x2_at) <= p[1]:
                        return
                cgen_pending[0] = next(cgen, "DONE")

        ln_ss_at = []
        b1_tp_at = {}

        def emit_b1(tb):
            for u in range(4):
                t = 4 * tb + u

                @block.sync
                def _(sync, t=t, tb=tb):
                    if t >= 1:
                        SS.wait(sync, ln_ss_at[t - 1])
                    if t < 4:
                        STG.wait(sync, STG.n)
                    LDXP.incd(sync.dma_start(
                        out=x1t[t % 2][:],
                        in_=x1_s[128 * t:128 * (t + 1), :]), t)

                pre_v = [(LDXP.c[t % 2], LDXP.c[t % 2].n)]
                pre_s = [(MM, MM.n)] if t > 0 else []
                emit_ln_tile(x1t[t % 2][:], xn2st[:], pre_waits_v=pre_v,
                             pre_waits_s=pre_s)
                ln_ss_at.append(SS.n)
                pre_t = [(SS, SS.n)]
                if tb == 0 and u == 0:
                    pre_t += [(VV, VV.n), (EV, EV.n)]
                if tb >= 2:
                    pre_t += [(MM, gate_mm_at.get(tb - 2, 0))]
                emit_transpose_chunks(
                    xn2st[:], ND,
                    lambda j, u=u, tb=tb: xn2T[tb % 2][:, 512 * j + 128 * u:
                                                       512 * j + 128 * (u + 1)],
                    ps_tpF, pre=pre_t)
            b1_tp_at[tb] = FR.n

        gate_mm_at = {}
        gslab_used = [0, 0]
        dslab_used = [0, 0]
        fuse_vv = []
        dn_ev = []
        emit_b1(0)
        for tb in range(4):
            if tb < 3:
                emit_b1(tb + 1)
            fuse_done = []
            for i in range(NI0):
                @block.gpsimd
                def _(gpsimd, tb=tb, i=i):
                    if gslab_used[i % 2]:
                        MM.wait(gpsimd, gslab_used[i % 2])
                    else:
                        SS.wait(gpsimd)
                        STG.wait(gpsimd)
                        FR.wait(gpsimd, b1_tp_at[0])
                    gv = gw_d.rearrange("(dc dp) o -> dp dc o", dp=128)
                    uv = uw_d.rearrange("(dc dp) o -> dp dc o", dp=128)
                    LDWG.incd(gpsimd.dma_start(
                        out=g8[i % 2][:].rearrange("p (dc o) -> p dc o", dc=ND),
                        in_=gv[:, :, 128 * i:128 * (i + 1)]), i)
                    LDWG.incd(gpsimd.dma_start(
                        out=u8[i % 2][:].rearrange("p (dc o) -> p dc o", dc=ND),
                        in_=uv[:, :, 128 * i:128 * (i + 1)]), i)

                @block.tensor
                def _(tensor, tb=tb, i=i):
                    LDWG.wait(tensor, i)
                    FR.wait(tensor, b1_tp_at[tb])
                    if len(fuse_done) >= 2:
                        VV.wait(tensor, fuse_done[-2])
                    elif fuse_vv:
                        VV.wait(tensor, fuse_vv[-1])
                    for d in range(ND):
                        tensor.matmul(ps_gB[i % 2][:],
                                      g8[i % 2][:, 128 * d:128 * (d + 1)],
                                      xn2T[tb % 2][:, 512 * d:512 * (d + 1)],
                                      start=(d == 0), stop=(d == ND - 1),
                                      skip_group_check=True)
                    for d in range(ND):
                        ins = tensor.matmul(
                            ps_uB[i % 2][:], u8[i % 2][:, 128 * d:128 * (d + 1)],
                            xn2T[tb % 2][:, 512 * d:512 * (d + 1)],
                            start=(d == 0), stop=(d == ND - 1),
                            skip_group_check=True)
                    MM.inc(ins)
                gslab_used[i % 2] = MM.n

                @block.scalar
                def _(scalar, tb=tb, i=i):
                    MM.wait(scalar)
                    if fuse_done:
                        VV.wait(scalar, fuse_done[-1])
                    elif fuse_vv:
                        VV.wait(scalar, fuse_vv[-1])
                    SS.inc(scalar.activation(out=gact[:], in_=ps_gB[i % 2][:],
                                             func=AF.Silu,
                                             scale=gs_sb[:, i:i + 1]))

                @block.vector
                def _(vector, tb=tb, i=i):
                    SS.wait(vector)
                    vector.scalar_tensor_tensor(
                        out=fuse[:, 512 * i:512 * (i + 1)],
                        in0=ps_uB[i % 2][:], scalar=us_sb[:, i:i + 1],
                        in1=gact[:], op0=OP.mult, op1=OP.mult)
                    vector.drain()
                    VV.inc(vector.engine_nop())
                fuse_done.append(VV.n)
                drive_c(6)
            fuse_vv.append(fuse_done[-1])
            gate_mm_at[tb] = MM.n

            for do in range(ND):
                @block.gpsimd
                def _(gpsimd, tb=tb, do=do):
                    if dslab_used[do % 2]:
                        MM.wait(gpsimd, dslab_used[do % 2])
                    else:
                        SS.wait(gpsimd)
                        STG.wait(gpsimd)
                        FR.wait(gpsimd, b1_tp_at[0])
                    dv = dw_d.rearrange("(ic ip) o -> ip ic o", ip=128)
                    LDWB.incd(gpsimd.dma_start(
                        out=d8[do % 2][:].rearrange("p (ic o) -> p ic o", ic=NI0),
                        in_=dv[:, :, 128 * do:128 * (do + 1)]), do)

                @block.sync
                def _(sync, tb=tb, do=do):
                    if dn_ev:
                        VV.wait(sync, dn_ev[-1])
                    else:
                        STG.wait(sync, STG.n)
                    for u in range(4):
                        t = 4 * tb + u
                        LDX.incd(sync.dma_start(
                            out=x1rs[u][:],
                            in_=x1_s[128 * t:128 * (t + 1),
                                     128 * do:128 * (do + 1)]))

                @block.tensor
                def _(tensor, tb=tb, do=do):
                    LDWB.wait(tensor, do)
                    VV.wait(tensor, fuse_done[-1])
                    sh_ev[do % 2].wait(tensor)
                    ps = ps_sh[do % 2]
                    for ic in range(NI0):
                        ins = tensor.matmul(
                            ps[:], d8[do % 2][:, 128 * ic:128 * (ic + 1)],
                            fuse[:, 512 * ic:512 * (ic + 1)],
                            start=(ic == 0), stop=(ic == NI0 - 1))
                    MM.inc(ins)

                @block.scalar
                def _(scalar, tb=tb, do=do):
                    MM.wait(scalar)
                    TT.wait(scalar)
                    EV.inc(scalar.activation(out=stgd[do % 2][:],
                                             in_=ps_sh[do % 2][:],
                                             func=AF.Identity,
                                             scale=ds_sb[:, do:do + 1]))
                    sh_ev[do % 2].inc(scalar.nop())

                @block.tensor
                def _(tensor, tb=tb, do=do):
                    EV.wait(tensor)
                    if dn_ev:
                        VV.wait(tensor, dn_ev[-1])
                    else:
                        VV.wait(tensor)
                    for u in range(4):
                        ins = tensor.transpose(ps_otF[:, 128 * u:128 * (u + 1)],
                                               stgd[do % 2][:, 128 * u:128 * (u + 1)],
                                               ident[:])
                    TT.inc(ins)

                @block.vector
                def _(vector, tb=tb, do=do):
                    TT.wait(vector)
                    LDX.wait(vector)
                    STG.wait(vector)
                    for u in range(4):
                        vector.scalar_tensor_tensor(
                            out=xpcF[u][:], in0=ps_otF[:, 128 * u:128 * (u + 1)],
                            scalar=1.0, in1=x1rs[u][:],
                            op0=OP.mult, op1=OP.add)
                    vector.drain()
                    VV.inc(vector.engine_nop())
                dn_ev.append(VV.n)

                @block.sync
                def _(sync, tb=tb, do=do):
                    VV.wait(sync)
                    for u in range(4):
                        t = 4 * tb + u
                        STG.incd(sync.dma_start(
                            out=x2_s[128 * t:128 * (t + 1),
                                     128 * do:128 * (do + 1)],
                            in_=xpcF[u][:]))
                drive_c(6)
            b_x2_at.append(STG.n)

        while cgen_pending[0] != "DONE":
            p = cgen_pending[0]
            cgen_pending[0] = next(cgen, "DONE")
        phBC.close()

    if upto >= 4:
        phD = ExitStack()
        sbD = lambda nm, shp, dt: phD.enter_context(nc.sbuf_tensor(nm, shp, dt))
        xn4T = sbD("xn4T", [128, ND * S], BF16)
        fuseM = sbD("fuseM", [128, NIM * S], BF16)
        routT = sbD("routT", [128, ND * 8], BF16)
        lgall = sbD("lgall", [128, NT * 8], F32)
        top8 = sbD("top8", [128, 8], F32)
        wgt = sbD("wgt", [128, 1], F32)
        cmb = sbD("cmb", [128, NT * 8], F32)
        rb16 = sbD("rb16", [8, D], BF16)

        c_st_at = STC.n
        phD1 = ExitStack()
        x3t = [phD1.enter_context(nc.sbuf_tensor(f"x3t{i}", [128, D], F32))
               for i in range(2)]
        xn4 = phD1.enter_context(nc.sbuf_tensor("xn4", [128, D], BF16))
        ps_tpD = phD1.enter_context(nc.psum_tensor("ps_tpD", [128, 512], BF16))
        ps_lg = phD1.enter_context(nc.psum_tensor("ps_lg", [128, 8], F32))

        @block.gpsimd
        def _(gpsimd):
            MM.wait(gpsimd)
            VV.wait(gpsimd)
            SS.wait(gpsimd)
            STG.wait(gpsimd)
            STC.wait(gpsimd)
            LDCG.incd(gpsimd.dma_start(out=rb16[:], in_=rw_d))
        rw_at = LDCG.n
        for j in range(ND):
            @block.tensor
            def _(tensor, j=j):
                LDCG.wait(tensor, rw_at)
                FR.wait(tensor)
                MM.inc(tensor.transpose(ps_tpD[:, 0:8],
                                        rb16[:, 128 * j:128 * (j + 1)],
                                        ident[0:8, 0:8]))

            @block.vector
            def _(vector, j=j):
                MM.wait(vector)
                vector.tensor_copy(out=routT[:, 8 * j:8 * (j + 1)],
                                   in_=ps_tpD[:, 0:8])
                vector.drain()
                FR.inc(vector.engine_nop())
        routT_at = FR.n

        xn_done_at = []
        for t in range(NT):
            @block.sync
            def _(sync, t=t):
                STC.wait(sync, c_st_at)
                if t >= 2:
                    SS.wait(sync, xn_done_at[t - 2])
                LDXP.incd(sync.dma_start(out=x3t[t % 2][:],
                                         in_=x3_s[128 * t:128 * (t + 1), :]), t)

            pw_s = [(MM, MM.n)] if t > 0 else []
            emit_ln_tile(x3t[t % 2][:], xn4[:],
                         pre_waits_v=[(LDXP.c[t % 2], LDXP.c[t % 2].n)],
                         pre_waits_s=pw_s)
            xn_done_at.append(SS.n)
            pre_t = [(SS, SS.n)] + ([(VV, VV.n), (EV, EV.n)] if t == 0 else [])
            emit_transpose_chunks(
                xn4[:], ND,
                lambda j, t=t: xn4T[:, j * S + 128 * t:j * S + 128 * t + 128],
                ps_tpD, pre=pre_t)

            @block.tensor
            def _(tensor, t=t):
                FR.wait(tensor, routT_at)
                FR.wait(tensor)
                LG.wait(tensor)
                for d in range(ND):
                    ins = tensor.matmul(
                        ps_lg[:], xn4T[:, d * S + 128 * t:d * S + 128 * (t + 1)],
                        routT[:, 8 * d:8 * (d + 1)],
                        start=(d == 0), stop=(d == ND - 1))
                MM.inc(ins)

            @block.vector
            def _(vector, t=t):
                MM.wait(vector)
                vector.tensor_copy(out=lgall[:, 8 * t:8 * (t + 1)], in_=ps_lg[:])
                vector.drain()
                vector.max(out=top8[:], in_=lgall[:, 8 * t:8 * (t + 1)])
                vector.drain()
                vector.tensor_tensor(out=wgt[:], in0=top8[:, 1:2],
                                     in1=top8[:, 0:1], op=OP.subtract)
                vector.drain()
                vector.drain()
                LG.inc(vector.engine_nop())

            @block.scalar
            def _(scalar, t=t):
                LG.wait(scalar)
                scalar.activation(out=wgt[:], in_=wgt[:], func=AF.Exp)
                scalar.drain()
                SS.inc(scalar.nop())

            @block.vector
            def _(vector, t=t):
                SS.wait(vector)
                vector.tensor_scalar(out=wgt[:], in0=wgt[:], scalar1=1.0,
                                     scalar2=None, op0=OP.add)
                vector.drain()
                vector.reciprocal(out=wgt[:], in_=wgt[:])
                vector.drain()
                vector.tensor_scalar(out=cmb[:, 8 * t:8 * (t + 1)],
                                     in0=lgall[:, 8 * t:8 * (t + 1)],
                                     scalar1=top8[:, 0:1], scalar2=wgt[:, 0:1],
                                     op0=OP.is_equal, op1=OP.mult)
                vector.drain()
                vector.drain()
                LG.inc(vector.engine_nop())
        routing_at = LG.n
        phD1.close()

        phE = ExitStack()
        gwb = [phE.enter_context(nc.sbuf_tensor(f"gwb{i}", [128, ND * 128], BF16))
               for i in range(2)]
        uwb = [phE.enter_context(nc.sbuf_tensor(f"uwb{i}", [128, ND * 128], BF16))
               for i in range(2)]
        gactM = phE.enter_context(nc.sbuf_tensor("gactM", [128, 512], BF16))
        dwb = [phE.enter_context(nc.sbuf_tensor(f"dwb{i}", [128, NIM * 128], BF16))
               for i in range(3)]
        stgm = [phE.enter_context(nc.sbuf_tensor(f"stgm{i}", [128, 512], BF16))
                for i in range(2)]
        x3strip = phE.enter_context(nc.sbuf_tensor("x3strip", [128, NT * 128], F32))
        ostrip = phE.enter_context(nc.sbuf_tensor("ostrip", [128, NT * 128], F32))
        ps_gM = [phE.enter_context(nc.psum_tensor(f"ps_gM{i}", [128, 512], F32))
                 for i in range(2)]
        ps_uM = [phE.enter_context(nc.psum_tensor(f"ps_uM{i}", [128, 512], F32))
                 for i in range(2)]
        ps_dM = [phE.enter_context(nc.psum_tensor(f"ps_dM{i}", [128, 512], F32))
                 for i in range(2)]
        ps_otM = phE.enter_context(nc.psum_tensor("ps_otM", [128, 512], BF16))

        slab_used = []
        dslab_used = {}
        fuse_done = []
        fuse_tb = {}
        dn_mm_prev = 0
        prev_store_at = ACC.n
        acc_prev = None
        strip_vv = []
        for e in range(NE):
            for i in range(NIM):
                @block.gpsimd
                def _(gpsimd, e=e, i=i):
                    if len(slab_used) >= 2:
                        MM.wait(gpsimd, slab_used[-2])
                    else:
                        MM.wait(gpsimd)
                        VV.wait(gpsimd)
                        SS.wait(gpsimd)
                        STG.wait(gpsimd)
                        STC.wait(gpsimd)
                    gv = mgw_d[e].rearrange("(hc hp) o -> hp hc o", hp=128)
                    uv = muw_d[e].rearrange("(hc hp) o -> hp hc o", hp=128)
                    LDWM.incd(gpsimd.dma_start(
                        out=gwb[i % 2][:].rearrange("p (hc o) -> p hc o", hc=ND),
                        in_=gv[:, :, 128 * i:128 * (i + 1)]), i)
                    LDWM.incd(gpsimd.dma_start(
                        out=uwb[i % 2][:].rearrange("p (hc o) -> p hc o", hc=ND),
                        in_=uv[:, :, 128 * i:128 * (i + 1)]), i)

                for tb in range(4):
                    bk = tb % 2

                    @block.tensor
                    def _(tensor, e=e, i=i, tb=tb, bk=bk):
                        LDWM.wait(tensor, i)
                        LG.wait(tensor, routing_at)
                        if fuse_tb.get(bk):
                            VV.wait(tensor, fuse_tb[bk][-1])
                        for d in range(ND):
                            tensor.matmul(ps_gM[bk][:],
                                          gwb[i % 2][:, 128 * d:128 * (d + 1)],
                                          xn4T[:, d * S + 512 * tb:d * S + 512 * (tb + 1)],
                                          start=(d == 0), stop=(d == ND - 1),
                                          skip_group_check=True)
                        for d in range(ND):
                            ins = tensor.matmul(
                                ps_uM[bk][:],
                                uwb[i % 2][:, 128 * d:128 * (d + 1)],
                                xn4T[:, d * S + 512 * tb:d * S + 512 * (tb + 1)],
                                start=(d == 0), stop=(d == ND - 1),
                                skip_group_check=True)
                        MM.inc(ins)

                    @block.scalar
                    def _(scalar, i=i, tb=tb, bk=bk):
                        MM.wait(scalar)
                        if fuse_done:
                            VV.wait(scalar, fuse_done[-1])
                        SS.inc(scalar.activation(out=gactM[:], in_=ps_gM[bk][:],
                                                 func=AF.Silu))

                    @block.vector
                    def _(vector, e=e, i=i, tb=tb, bk=bk):
                        SS.wait(vector)
                        MM.wait(vector, dn_mm_prev)
                        vector.scalar_tensor_tensor(
                            out=fuseM[:, i * S + 512 * tb:i * S + 512 * (tb + 1)],
                            in0=ps_uM[bk][:], scalar=1.0, in1=gactM[:],
                            op0=OP.mult, op1=OP.mult)
                        vector.drain()
                        VV.inc(vector.engine_nop())
                    fuse_tb.setdefault(bk, []).append(VV.n)
                    fuse_done.append(VV.n)
                slab_used.append(MM.n)
            gate_done_vv = VV.n

            def emit_dwb_load(hc, e=e):
                @block.gpsimd
                def _(gpsimd, e=e, hc=hc):
                    if dslab_used.get(hc % 3):
                        MM.wait(gpsimd, dslab_used[hc % 3])
                    else:
                        MM.wait(gpsimd)
                        VV.wait(gpsimd)
                        STC.wait(gpsimd)
                    dv = mdw_d[e].rearrange("(ic ip) o -> ip ic o", ip=128)
                    LDWD.incd(gpsimd.dma_start(
                        out=dwb[hc % 3][:].rearrange("p (ic o) -> p ic o", ic=NIM),
                        in_=dv[:, :, 128 * hc:128 * (hc + 1)]), hc)

            it = 0
            emit_dwb_load(0)
            emit_dwb_load(1)
            for hc in range(ND):
                if hc + 2 < ND:
                    emit_dwb_load(hc + 2)

                if e == 0:
                    @block.sync
                    def _(sync, hc=hc):
                        if strip_vv:
                            VV.wait(sync, strip_vv[-1])
                        x3v = x3_s.rearrange("(tt tp) h -> tp tt h", tp=128)
                        LDX.incd(sync.dma_start(
                            out=x3strip[:].rearrange("p (tt h) -> p tt h", tt=NT),
                            in_=x3v[:, :, 128 * hc:128 * (hc + 1)]))

                for tb in range(4):
                    @block.tensor
                    def _(tensor, e=e, hc=hc, tb=tb, it=it):
                        LDWD.wait(tensor, hc)
                        VV.wait(tensor, fuse_done[-1])
                        if EV.n >= 1:
                            EV.wait(tensor, EV.n - 1)
                        ps = ps_dM[it % 2]
                        for ic in range(NIM):
                            ins = tensor.matmul(
                                ps[:], dwb[hc % 3][:, 128 * ic:128 * (ic + 1)],
                                fuseM[:, ic * S + 512 * tb:ic * S + 512 * (tb + 1)],
                                start=(ic == 0), stop=(ic == NIM - 1))
                        MM.inc(ins)

                    @block.scalar
                    def _(scalar, hc=hc, tb=tb, it=it):
                        MM.wait(scalar)
                        TT.wait(scalar)
                        EV.inc(scalar.activation(out=stgm[it % 2][:],
                                                 in_=ps_dM[it % 2][:],
                                                 func=AF.Copy))

                    @block.tensor
                    def _(tensor, hc=hc, tb=tb, it=it):
                        EV.wait(tensor)
                        if strip_vv:
                            VV.wait(tensor, strip_vv[-1])
                        for u in range(4):
                            ins = tensor.transpose(
                                ps_otM[:, 128 * u:128 * (u + 1)],
                                stgm[it % 2][:, 128 * u:128 * (u + 1)],
                                ident[:])
                        TT.inc(ins)

                    @block.vector
                    def _(vector, e=e, hc=hc, tb=tb):
                        TT.wait(vector)
                        if e == 0:
                            LDX.wait(vector)
                        ACC.wait(vector, prev_store_at)
                        for u in range(4):
                            t = 4 * tb + u
                            csc = cmb[:, 8 * t + e:8 * t + e + 1]
                            if e == 0:
                                vector.scalar_tensor_tensor(
                                    out=ostrip[:, 128 * t:128 * (t + 1)],
                                    in0=ps_otM[:, 128 * u:128 * (u + 1)],
                                    scalar=csc,
                                    in1=x3strip[:, 128 * t:128 * (t + 1)],
                                    op0=OP.mult, op1=OP.add)
                            else:
                                vector.tensor_scalar(
                                    out=ostrip[:, 128 * t:128 * (t + 1)],
                                    in0=ps_otM[:, 128 * u:128 * (u + 1)],
                                    scalar1=csc, scalar2=None, op0=OP.mult)
                        vector.drain()
                        VV.inc(vector.engine_nop())
                    strip_vv.append(VV.n)
                    it += 1
                dslab_used[hc % 3] = MM.n

                @block.gpsimd
                def _(gpsimd, e=e, hc=hc):
                    VV.wait(gpsimd)
                    if e > 0 and acc_prev is not None:
                        ACC.wait(gpsimd, acc_prev[hc])
                    ov = out_d.rearrange("(tt tp) h -> tp tt h", tp=128)
                    ins = gpsimd.dma_start(
                        out=ov[:, :, 128 * hc:128 * (hc + 1)],
                        in_=ostrip[:].rearrange("p (tt h) -> p tt h", tt=NT),
                        **({"accum_op": OP.add} if e > 0 else {}))
                    ACC.incd(ins)
                prev_store_at = ACC.n
            acc_prev = {hc: ACC.n - 16 * (ND - 1 - hc) for hc in range(ND)}
            dn_mm_prev = MM.n
        phE.close()
        phD.close()
    else:
        src = {1: x1_s, 2: x2_s, 3: x3_s}[upto]

        @block.sync
        def _(sync):
            ST.wait(sync)
            STG.wait(sync)
            ST.incd(sync.dma_start(out=out_d, in_=src))

    if taps:
        @block.sync
        def _(sync):
            ST.wait(sync)
            STG.wait(sync)
            ACC.wait(sync)
            for tnm in taps:
                ST.incd(sync.dma_start(out=tap_out[tnm], in_=tapt[tnm]))

    @block.sync
    def _(sync):
        ST.wait(sync)
        STG.wait(sync)
        ACC.wait(sync)

    ctx.close()
    return nc


def _host_consts():
    ident = np.eye(128, dtype=np.float32).astype(ml_dtypes.bfloat16)
    iota2 = np.broadcast_to(np.arange(S, dtype=np.float32), (2, S)).copy()
    init4 = np.ones((4, S), dtype=np.float32).astype(ml_dtypes.bfloat16)
    jj = np.arange(128)
    mqk = np.where(jj[None, :] > jj[:, None], NEG, 0.0).astype(np.float32)
    mkq = np.where(jj[:, None] > jj[None, :], NEG, 0.0).astype(np.float32)
    onesb = np.ones((1, 512), dtype=np.float32).astype(ml_dtypes.bfloat16)
    onesc = np.ones((128, 1), dtype=np.float32).astype(ml_dtypes.bfloat16)
    return dict(ident=ident, iota2=iota2, init4=init4, mqk=mqk, mkq=mkq,
                onesb=onesb, onesc=onesc)


def prepare_maps(inputs):
    x = np.ascontiguousarray(np.asarray(inputs["L0_attn_x"], dtype=np.float32))

    def row(a):
        return np.ascontiguousarray(np.asarray(a, np.float32).reshape(1, -1))

    def colsq(a, ncol):
        return np.ascontiguousarray(np.asarray(a, np.float32).reshape(ncol, 128).T)

    w2 = np.zeros((16, 128, 384), dtype=np.float32)
    wqkv = np.asarray(inputs["L1_attn_Wqkv"], np.float32)
    for p in range(16):
        w2[p, 0:64, 0:192] = wqkv[2 * p]
        w2[p, 64:128, 192:384] = wqkv[2 * p + 1]
    bqkv2 = np.ascontiguousarray(
        np.asarray(inputs["L1_attn_bqkv"], np.float32).reshape(1, -1))
    wout = np.asarray(inputs["L1_attn_Wout"], np.float32)
    woutbd = np.zeros((128, 128), dtype=np.float32)
    woutbd[0:64, 0:64] = wout
    woutbd[64:128, 64:128] = wout
    bout2 = np.ascontiguousarray(
        np.tile(np.asarray(inputs["L1_attn_bout"], np.float32), 2).reshape(1, -1))

    common = dict(
        wq=np.ascontiguousarray(np.asarray(inputs["L0_attn_wq"], np.float32)),
        wk=np.ascontiguousarray(np.asarray(inputs["L0_attn_wk"], np.float32)),
        wv=np.ascontiguousarray(np.asarray(inputs["L0_attn_wv"], np.float32)),
        wo=np.ascontiguousarray(np.asarray(inputs["L0_attn_wo"], np.float32)),
        bq=row(inputs["L0_attn_bq"]), bk=row(inputs["L0_attn_bk"]),
        bv=row(inputs["L0_attn_bv"]), bo=row(inputs["L0_attn_bo"]),
        gw=np.ascontiguousarray(np.asarray(inputs["L0_ffn_gate_w"], np.float32)),
        uw=np.ascontiguousarray(np.asarray(inputs["L0_ffn_up_w"], np.float32)),
        dw=np.ascontiguousarray(np.asarray(inputs["L0_ffn_down_w"], np.float32)),
        gs=colsq(inputs["L0_ffn_gate_scale"], NI0),
        us=colsq(inputs["L0_ffn_up_scale"], NI0),
        ds=colsq(inputs["L0_ffn_down_scale"], ND),
        w2=w2, bqkv2=bqkv2, woutbd=woutbd, bout2=bout2,
        rw=np.ascontiguousarray(np.asarray(inputs["L1_ffn_router_w"], np.float32)),
        mgw=np.ascontiguousarray(
            np.asarray(inputs["L1_ffn_gate_w"], np.float32).transpose(0, 2, 1)),
        muw=np.ascontiguousarray(
            np.asarray(inputs["L1_ffn_up_w"], np.float32).transpose(0, 2, 1)),
        mdw=np.ascontiguousarray(
            np.asarray(inputs["L1_ffn_down_w"], np.float32).transpose(0, 2, 1)),
        **_host_consts(),
    )
    return common, x


def kernel(**inputs):
    common, x = prepare_maps(inputs)
    nc = build_program()
    in_maps = [dict(common, x0=np.ascontiguousarray(x[c])) for c in range(8)]
    res = run_bass_kernel_spmd(nc, in_maps, core_ids=list(range(8)))
    return np.stack([np.asarray(res.results[c]["out"]) for c in range(8)], axis=0)


# version 09 (final)
# speedup vs baseline: 1.0359x; 1.0359x over previous
import numpy as np
import ml_dtypes
from contextlib import ExitStack

import concourse.bass as bass
import concourse.mybir as mybir
from concourse.alu_op_type import AluOpType as OP
from concourse.bass_utils import run_bass_kernel_spmd

F32 = mybir.dt.float32
BF16 = mybir.dt.bfloat16
F8 = mybir.dt.float8e4
AF = mybir.ActivationFunctionType
AX = mybir.AxisListType

S = 2048
D = 2048
NT = S // 128
ND = D // 128
NH = 16
I0 = 6144
NI0 = I0 // 128
IM = 1408
NIM = IM // 128
NE = 8
EPS = 1e-5
INV_SQRT_HD = float(1.0 / np.sqrt(128.0))
NEG = -1e30

UPTO = 4
DEBUG_TAPS = ()


def _alibi_slopes(n_heads):
    start = 2.0 ** (-8.0 / n_heads)
    return [start * start ** i for i in range(n_heads)]


class Ctr:
    def __init__(self, nc, ctx, name):
        self.h = ctx.enter_context(nc.semaphore(name))
        self.n = 0

    def inc(self, instr, k=1):
        instr.then_inc(self.h, k)
        self.n += k
        return instr

    def incd(self, instr):
        return self.inc(instr, 16)

    def wait(self, eng, at=None):
        v = self.n if at is None else at
        if v > 0:
            eng.wait_ge(self.h, v)


class PCtr:

    def __init__(self, nc, ctx, name, k=2):
        self.c = [Ctr(nc, ctx, f"{name}{i}") for i in range(k)]
        self.k = k

    def incd(self, instr, i):
        return self.c[i % self.k].incd(instr)

    def wait(self, eng, i, at=None):
        self.c[i % self.k].wait(eng, at)

    def wait_all(self, eng):
        for c in self.c:
            c.wait(eng)


def build_program(upto=None, taps=None, probe=None):
    upto = UPTO if upto is None else upto
    taps = DEBUG_TAPS if taps is None else taps
    nc = bass.Bass("TRN2", target_bir_lowering=False)

    dt_in = lambda nm, shp: nc.dram_tensor(nm, shp, F32, kind="ExternalInput").ap()
    x0_d = dt_in("x0", [S, D])
    wq_d, wk_d, wv_d, wo_d = (dt_in(n, [D, D]) for n in ["wq", "wk", "wv", "wo"])
    bq_d, bk_d, bv_d, bo_d = (dt_in(n, [1, D]) for n in ["bq", "bk", "bv", "bo"])
    gw_d, uw_d = dt_in("gw", [D, I0]), dt_in("uw", [D, I0])
    dw_d = dt_in("dw", [I0, D])
    gs_d, us_d = dt_in("gs", [128, NI0]), dt_in("us", [128, NI0])
    ds_d = dt_in("ds", [128, ND])
    w2_d = dt_in("w2", [16, 128, 384])
    bqkv2_d = dt_in("bqkv2", [1, 16 * 384])
    woutbd_d = dt_in("woutbd", [128, 128])
    bout2_d = dt_in("bout2", [1, 128])
    rw_d = dt_in("rw", [8, D])
    mgw_d = dt_in("mgw", [NE, D, IM])
    muw_d = dt_in("muw", [NE, D, IM])
    mdw_d = dt_in("mdw", [NE, IM, D])
    ident_d = nc.dram_tensor("ident", [128, 128], BF16, kind="ExternalInput").ap()
    iota2_d = dt_in("iota2", [2, S])
    init4_d = nc.dram_tensor("init4", [4, S], BF16, kind="ExternalInput").ap()
    mqk_d = dt_in("mqk", [128, 128])
    mkq_d = dt_in("mkq", [128, 128])
    onesb_d = nc.dram_tensor("onesb", [1, 512], BF16, kind="ExternalInput").ap()
    onesc_d = nc.dram_tensor("onesc", [128, 1], BF16, kind="ExternalInput").ap()

    out_d = nc.dram_tensor("out", [S, D], F32, kind="ExternalOutput").ap()

    q_s = nc.dram_tensor("q_s", [D, S], BF16, kind="Internal").ap()
    k_s = nc.dram_tensor("k_s", [D, S], BF16, kind="Internal").ap()
    v_s = nc.dram_tensor("v_s", [D, S], BF16, kind="Internal").ap()
    x1_s = nc.dram_tensor("x1_s", [S, D], F32, kind="Internal").ap()
    x2_s = nc.dram_tensor("x2_s", [S, D], F32, kind="Internal").ap()
    x3_s = nc.dram_tensor("x3_s", [S, D], F32, kind="Internal").ap()
    tapt = {"x1_s": x1_s, "x2_s": x2_s, "x3_s": x3_s}
    tap_out = {t: nc.dram_tensor("tap_" + t, [S, D], F32, kind="ExternalOutput").ap()
               for t in taps}

    ctx = ExitStack()
    sb = lambda name, shape, dt: ctx.enter_context(nc.sbuf_tensor(name, shape, dt))

    ident = sb("identb", [128, 128], BF16)
    onesb = sb("onesbb", [1, 512], BF16)
    onesc = sb("onescb", [128, 1], BF16)
    gs_sb = sb("gs_sb", [128, NI0], F32)
    us_sb = sb("us_sb", [128, NI0], F32)
    ds_sb = sb("ds_sb", [128, ND], F32)
    st6 = sb("st6", [128, 24], F32)
    mv2 = sb("mv2", [128, 2], F32)
    tmp1 = sb("tmp1", [128, 1], F32)
    istd = sb("istd", [128, 1], F32)
    nmi = sb("nmi", [128, 1], F32)

    block = ctx.enter_context(nc.Block())

    sem_names = ["ldc", "ldcg", "ldx", "ldw", "mm", "ev", "vv", "ss", "tt",
                 "st", "stg", "ldh", "ph", "fr", "acc", "aux", "lg"]
    C = {nm: Ctr(nc, ctx, nm) for nm in sem_names}
    LDC, LDCG, LDX, LDW, MM, EV, VV, SS, TT, ST, STG = (C[n] for n in
        ["ldc", "ldcg", "ldx", "ldw", "mm", "ev", "vv", "ss", "tt", "st", "stg"])
    LDH, PH, FR, ACC, AUX, LG = (C[n] for n in ["ldh", "ph", "fr", "acc", "aux", "lg"])
    LDXP = PCtr(nc, ctx, "ldxp", 2)
    LDWP = PCtr(nc, ctx, "ldwp", 2)
    LDWD = PCtr(nc, ctx, "ldwd", 3)
    LDWG = PCtr(nc, ctx, "ldwg", 2)
    LDWB = PCtr(nc, ctx, "ldwb", 2)
    LDWM = PCtr(nc, ctx, "ldwm", 2)
    STP = PCtr(nc, ctx, "stp", 4)

    @block.sync
    def _(sync):
        LDC.incd(sync.dma_start(out=ident[:], in_=ident_d))
        LDC.incd(sync.dma_start(out=onesb[:], in_=onesb_d))
        LDC.incd(sync.dma_start(out=onesc[:], in_=onesc_d))
        LDC.incd(sync.dma_start(out=gs_sb[:], in_=gs_d))
        LDC.incd(sync.dma_start(out=us_sb[:], in_=us_d))
        LDC.incd(sync.dma_start(out=ds_sb[:], in_=ds_d))

    def emit_ln_tile(x_ap, out_ap, pre_waits_v=(), pre_waits_s=()):
        @block.vector
        def _(vector):
            for cn, at in pre_waits_v:
                cn.wait(vector, at)
            SS.wait(vector)
            for j in range(4):
                vector.bn_stats(out=st6[:, 6 * j:6 * j + 6],
                                in_=x_ap[:, 512 * j:512 * (j + 1)])
            vector.drain()
            vector.bn_aggr(out=mv2[:], in_=st6[:])
            vector.drain()
            vector.tensor_scalar(out=tmp1[:], in0=mv2[:, 1:2], scalar1=EPS,
                                 scalar2=None, op0=OP.add)
            vector.drain()
            vector.drain()
            VV.inc(vector.engine_nop())

        @block.scalar
        def _(scalar):
            VV.wait(scalar)
            scalar.activation(out=tmp1[:], in_=tmp1[:], func=AF.Ln)
            scalar.drain()
            scalar.activation(out=istd[:], in_=tmp1[:], func=AF.Exp, scale=-0.5)
            scalar.drain()
            SS.inc(scalar.nop())

        @block.vector
        def _(vector):
            SS.wait(vector)
            vector.tensor_tensor(out=nmi[:], in0=mv2[:, 0:1], in1=istd[:],
                                 op=OP.mult)
            vector.drain()
            vector.tensor_scalar(out=nmi[:], in0=nmi[:], scalar1=-1.0,
                                 scalar2=None, op0=OP.mult)
            vector.drain()
            vector.drain()
            VV.inc(vector.engine_nop())

        @block.scalar
        def _(scalar):
            VV.wait(scalar)
            for cn, at in pre_waits_s:
                cn.wait(scalar, at)
            SS.inc(scalar.activation(out=out_ap, in_=x_ap, func=AF.Identity,
                                     scale=istd[:, :], bias=nmi[:, :]))

    def emit_transpose_chunks(src_ap, n, dst_fn, ps_tp, pre=()):
        for g in range((n + 3) // 4):
            cnt = min(4, n - 4 * g)

            @block.tensor
            def _(tensor, g=g, cnt=cnt):
                for cn, at in pre:
                    cn.wait(tensor, at)
                FR.wait(tensor)
                for u in range(cnt):
                    j = 4 * g + u
                    ins = tensor.transpose(ps_tp[:, 128 * u:128 * (u + 1)],
                                           src_ap[:, 128 * j:128 * (j + 1)],
                                           ident[:])
                    if u == cnt - 1:
                        MM.inc(ins)

            @block.vector
            def _(vector, g=g, cnt=cnt):
                MM.wait(vector)
                for u in range(cnt):
                    j = 4 * g + u
                    vector.tensor_copy(out=dst_fn(j),
                                       in_=ps_tp[:, 128 * u:128 * (u + 1)])
                vector.drain()
                FR.inc(vector.engine_nop())

    phA = ExitStack()
    sbA = lambda nm, shp, dt: phA.enter_context(nc.sbuf_tensor(nm, shp, dt))
    iota2 = sbA("iota2b", [2, S], F32)
    mqk = sbA("mqkb", [128, 128], F32)
    mkq = sbA("mkqb", [128, 128], F32)
    brow = {nm: sbA(nm + "row", [1, D], BF16) for nm in ["bq", "bk", "bv", "bo"]}

    @block.sync
    def _(sync):
        LDC.incd(sync.dma_start(out=iota2[:], in_=iota2_d))
        LDC.incd(sync.dma_start(out=mqk[:], in_=mqk_d))
        LDC.incd(sync.dma_start(out=mkq[:], in_=mkq_d))

    @block.gpsimd
    def _(gpsimd):
        for nm, dd in [("bq", bq_d), ("bk", bk_d), ("bv", bv_d), ("bo", bo_d)]:
            LDCG.incd(gpsimd.dma_start(out=brow[nm][:], in_=dd))
    brow_at = LDCG.n
    ldc_at = LDC.n

    phA1 = ExitStack()
    xt = [phA1.enter_context(nc.sbuf_tensor(f"xt{i}", [128, D], F32))
          for i in range(2)]
    xnst = phA1.enter_context(nc.sbuf_tensor("xnst", [128, D], BF16))
    xnT = phA1.enter_context(nc.sbuf_tensor("xnT", [128, ND * S], BF16))
    ps_tpA = phA1.enter_context(nc.psum_tensor("ps_tpA", [128, 512], BF16))

    xn_done_at = []
    tp_done_at = []
    for t in range(NT):
        @block.sync
        def _(sync, t=t):
            if t >= 2:
                SS.wait(sync, xn_done_at[t - 2])
            LDXP.incd(sync.dma_start(out=xt[t % 2][:],
                                     in_=x0_d[128 * t:128 * (t + 1), :]), t)

        pw_s = [(MM, MM.n)] if t > 0 else []
        emit_ln_tile(xt[t % 2][:], xnst[:],
                     pre_waits_v=[(LDXP.c[t % 2], LDXP.c[t % 2].n),
                                  (LDC, LDC.n)], pre_waits_s=pw_s)
        xn_done_at.append(SS.n)
        emit_transpose_chunks(
            xnst[:], ND,
            lambda j, t=t: xnT[:, j * S + 128 * t: j * S + 128 * t + 128],
            ps_tpA, pre=[(SS, SS.n)])
        tp_done_at.append(FR.n)

    xnT_at = FR.n

    if probe == "xnT":
        pout = nc.dram_tensor("probe0", [128, ND * S], F32,
                              kind="ExternalOutput").ap()

        @block.gpsimd
        def _(gpsimd):
            FR.wait(gpsimd, xnT_at)
            ACC.incd(gpsimd.dma_start(out=pout, in_=xnT[:]))

        @block.sync
        def _(sync):
            ACC.wait(sync)
        ctx.pop_all()
        return nc

    wbuf = [phA1.enter_context(nc.sbuf_tensor(f"wbufA{i}", [128, ND * 128], BF16))
            for i in range(2)]
    stgA = [phA1.enter_context(nc.sbuf_tensor(f"stgA{i}", [128, 512], BF16))
            for i in range(4)]
    ps_p = [phA1.enter_context(nc.psum_tensor(f"ps_p{i}", [128, 512], F32))
            for i in range(2)]

    slab_used = []

    def qkv_pass(w_dram, b_ap, dst, scale):
        for do in range(ND):
            @block.gpsimd
            def _(gpsimd, do=do):
                if len(slab_used) >= 2:
                    MM.wait(gpsimd, slab_used[-2])
                wv_ = w_dram.rearrange("(dc dp) o -> dp dc o", dp=128)
                LDWP.incd(gpsimd.dma_start(
                    out=wbuf[do % 2][:].rearrange("p (dc o) -> p dc o", dc=ND),
                    in_=wv_[:, :, 128 * do:128 * (do + 1)]), do)

            for tb in range(4):
                @block.tensor
                def _(tensor, do=do, tb=tb):
                    LDWP.wait(tensor, do)
                    LDC.wait(tensor, ldc_at)
                    LDCG.wait(tensor, brow_at)
                    FR.wait(tensor, xnT_at)
                    if EV.n >= 2:
                        EV.wait(tensor, EV.n - 1)
                    ps = ps_p[(4 * do + tb) % 2]
                    for d in range(ND):
                        tensor.matmul(ps[:],
                                      wbuf[do % 2][:, 128 * d:128 * (d + 1)],
                                      xnT[:, d * S + 512 * tb:d * S + 512 * (tb + 1)],
                                      start=(d == 0), stop=False)
                    MM.inc(tensor.matmul(
                        ps[:], b_ap[0:1, 128 * do:128 * (do + 1)],
                        onesb[0:1, 0:512], start=False, stop=True))

                @block.vector
                def _(vector, do=do, tb=tb):
                    MM.wait(vector)
                    STP.wait(vector, 4 * do + tb)
                    vector.tensor_scalar(out=stgA[(4 * do + tb) % 4][:],
                                         in0=ps_p[(4 * do + tb) % 2][:],
                                         scalar1=float(scale), scalar2=None,
                                         op0=OP.mult)
                    vector.drain()
                    EV.inc(vector.engine_nop())

                @block.sync
                def _(sync, do=do, tb=tb):
                    EV.wait(sync)
                    STP.incd(sync.dma_start(
                        out=dst[128 * do:128 * (do + 1), 512 * tb:512 * (tb + 1)],
                        in_=stgA[(4 * do + tb) % 4][:]), 4 * do + tb)
            slab_used.append(MM.n)

    qkv_pass(wq_d, brow["bq"], q_s, INV_SQRT_HD)
    qkv_pass(wk_d, brow["bk"], k_s, 1.0)
    qkv_pass(wv_d, brow["bv"], v_s, 1.0)
    a2_st_at = [STP.c[i].n for i in range(4)]

    if probe == "qkv":
        pq = nc.dram_tensor("probe_q", [D, S], F32, kind="ExternalOutput").ap()
        pk = nc.dram_tensor("probe_k", [D, S], F32, kind="ExternalOutput").ap()
        pv = nc.dram_tensor("probe_v", [D, S], F32, kind="ExternalOutput").ap()

        @block.gpsimd
        def _(gpsimd):
            for _i in range(4):
                STP.c[_i].wait(gpsimd, a2_st_at[_i])
            ACC.incd(gpsimd.dma_start(out=pq, in_=q_s))
            ACC.incd(gpsimd.dma_start(out=pk, in_=k_s))
            ACC.incd(gpsimd.dma_start(out=pv, in_=v_s))

        @block.sync
        def _(sync):
            ACC.wait(sync)
        ctx.pop_all()
        return nc
    phA1.close()

    st_att = ExitStack()
    att = [st_att.enter_context(nc.sbuf_tensor(f"att{t}", [128, D], BF16,
                                               side="right"))
           for t in range(NT)]
    phA3 = ExitStack()
    sb3 = lambda nm, shp, dt: phA3.enter_context(nc.sbuf_tensor(nm, shp, dt))
    qh = [sb3(f"qh{i}", [128, S], BF16) for i in range(2)]
    kh = [sb3(f"kh{i}", [128, S], BF16) for i in range(2)]
    vhf = [sb3(f"vhf{i}", [128, S], BF16) for i in range(2)]
    vtok = sb3("vtok", [128, NT * 129], BF16)
    tileA = [sb3(f"tileA{i}", [4, S], BF16) for i in range(2)]
    tileB = [sb3(f"tileB{i}", [4, S], BF16) for i in range(2)]
    alo_r = sb3("alo_r", [1, S], BF16)
    nbhi_r = sb3("nbhi_r", [1, S], BF16)
    nblo_r = sb3("nblo_r", [1, S], BF16)
    arow = sb3("arow", [1, S], F32)
    sqbuf = sb3("sqbuf", [128, S], BF16)
    krow = sb3("krow", [1, S], F32)
    qrow = sb3("qrow", [1, S], F32)
    kmx = sb3("kmx", [1, 1], F32)
    bnd = sb3("bnd", [1, S], F32)
    nmfull = sb3("nmfull", [1, S], BF16)
    rinv = sb3("rinv", [128, 1], F32)
    PT = sb3("PT", [128, 16 * 512], BF16)
    ps_tpB = phA3.enter_context(nc.psum_tensor("ps_tpB", [128, 512], BF16))
    ps_sT = [phA3.enter_context(nc.psum_tensor(f"ps_sT{i}", [128, 1024], F32))
             for i in range(2)]
    ps_av = [phA3.enter_context(nc.psum_tensor(f"ps_av{i}", [128, 512], F32))
             for i in range(2)]
    ps_bd = phA3.enter_context(nc.psum_tensor("ps_bd", [1, 512], F32))

    LDHP = PCtr(nc, ctx, "ldhp", 2)
    head_done_at = []
    head_mm_at = []
    aux_at = []
    avev = []
    prev_av_at = 0
    slopes_all = _alibi_slopes(NH)
    nh_build = 1 if probe == "att1" else NH

    @block.vector
    def _(vector):
        MM.wait(vector)
        for _i in range(4):
            STP.c[_i].wait(vector, a2_st_at[_i])
        for j in range(NT):
            vector.memset(vtok[:, 129 * j + 128:129 * j + 129], 1.0)
        vector.drain()
        VV.inc(vector.engine_nop())

    for h in range(nh_build):
        @block.sync
        def _(sync, h=h):
            if h == 0:
                for _i in range(4):
                    STP.c[_i].wait(sync, a2_st_at[_i])
                LDC.wait(sync)
                LDHP.incd(sync.dma_start(out=tileA[0][:], in_=init4_d), 0)
                LDHP.incd(sync.dma_start(out=tileA[1][:], in_=init4_d), 1)
                LDHP.incd(sync.dma_start(out=tileB[0][:], in_=init4_d), 0)
                LDHP.incd(sync.dma_start(out=tileB[1][:], in_=init4_d), 1)
            if h >= 2:
                MM.wait(sync, head_mm_at[h - 2])
            LDHP.incd(sync.dma_start(out=qh[h % 2][:],
                                     in_=q_s[128 * h:128 * (h + 1), :]), h)
            LDHP.incd(sync.dma_start(out=kh[h % 2][:],
                                     in_=k_s[128 * h:128 * (h + 1), :]), h)
            LDHP.incd(sync.dma_start(out=vhf[h % 2][:],
                                     in_=v_s[128 * h:128 * (h + 1), :]), h)

        slope = float(slopes_all[h])

        @block.scalar
        def _(scalar, h=h, slope=slope):
            LDHP.wait(scalar, h)
            if len(head_mm_at) >= 2:
                MM.wait(scalar, head_mm_at[h - 2])
            if aux_at:
                AUX.wait(scalar, aux_at[-1])
            scalar.activation(out=arow[:], in_=iota2[0:1, :], func=AF.Copy,
                              scale=slope)
            scalar.drain()
            scalar.activation(out=tileB[h % 2][0:1, :], in_=arow[:], func=AF.Copy)
            scalar.drain()
            SS.inc(scalar.nop())

        @block.vector
        def _(vector, h=h):
            SS.wait(vector)
            vector.tensor_tensor(out=alo_r[:], in0=arow[:],
                                 in1=tileB[h % 2][0:1, :], op=OP.subtract)
            vector.drain()
            VV.inc(vector.engine_nop())

        @block.scalar
        def _(scalar, h=h, slope=slope):
            VV.wait(scalar)
            scalar.activation(out=arow[:], in_=iota2[0:1, :], func=AF.Copy,
                              scale=-slope)
            scalar.drain()
            scalar.activation(out=nbhi_r[:], in_=arow[:], func=AF.Copy)
            scalar.drain()
            SS.inc(scalar.nop())

        @block.vector
        def _(vector, h=h):
            SS.wait(vector)
            vector.tensor_tensor(out=nblo_r[:], in0=arow[:],
                                 in1=nbhi_r[:], op=OP.subtract)
            vector.drain()
            VV.inc(vector.engine_nop())

        @block.gpsimd
        def _(gpsimd, h=h):
            VV.wait(gpsimd)
            if len(head_mm_at) >= 2:
                MM.wait(gpsimd, head_mm_at[h - 2])
            AUX.incd(gpsimd.dma_start(out=tileB[h % 2][1:2, :], in_=alo_r[:]))
            AUX.incd(gpsimd.dma_start(out=tileA[h % 2][2:3, :], in_=nbhi_r[:]))
            AUX.incd(gpsimd.dma_start(out=tileA[h % 2][3:4, :], in_=nblo_r[:]))
        aux_at.append(AUX.n)
        tiles_at = AUX.n

        emit_transpose_chunks(vhf[h % 2][:], NT,
                              lambda j: vtok[:, 129 * j:129 * j + 128],
                              ps_tpB, pre=[(LDHP.c[h % 2], LDHP.c[h % 2].n)])
        vtok_at = FR.n

        @block.vector
        def _(vector, h=h):
            LDHP.wait(vector, h)
            SS.wait(vector)
            vector.tensor_tensor(out=sqbuf[:], in0=kh[h % 2][:],
                                 in1=kh[h % 2][:], op=OP.mult)
            vector.drain()
            VV.inc(vector.engine_nop())
        for j in range(4):
            @block.tensor
            def _(tensor, j=j):
                VV.wait(tensor)
                SS.wait(tensor)
                MM.inc(tensor.matmul(ps_bd[:, :], onesc[:, :],
                                     sqbuf[:, 512 * j:512 * (j + 1)],
                                     start=True, stop=True))

            @block.vector
            def _(vector, j=j):
                MM.wait(vector)
                vector.tensor_copy(out=krow[0:1, 512 * j:512 * (j + 1)],
                                   in_=ps_bd[:, :])
                vector.drain()
                SS.inc(vector.engine_nop())

        @block.vector
        def _(vector, h=h):
            SS.wait(vector)
            vector.tensor_tensor(out=sqbuf[:], in0=qh[h % 2][:],
                                 in1=qh[h % 2][:], op=OP.mult)
            vector.drain()
            vector.tensor_reduce(out=kmx[:], in_=krow[:], axis=AX.X, op=OP.max)
            vector.drain()
            VV.inc(vector.engine_nop())
        for j in range(4):
            @block.tensor
            def _(tensor, j=j):
                VV.wait(tensor)
                SS.wait(tensor)
                MM.inc(tensor.matmul(ps_bd[:, :], onesc[:, :],
                                     sqbuf[:, 512 * j:512 * (j + 1)],
                                     start=True, stop=True))

            @block.vector
            def _(vector, j=j):
                MM.wait(vector)
                vector.tensor_copy(out=qrow[0:1, 512 * j:512 * (j + 1)],
                                   in_=ps_bd[:, :])
                vector.drain()
                SS.inc(vector.engine_nop())

        @block.vector
        def _(vector, h=h):
            SS.wait(vector)
            vector.tensor_scalar(out=bnd[:], in0=qrow[:], scalar1=kmx[0:1, 0:1],
                                 scalar2=1e-30, op0=OP.mult, op1=OP.add)
            vector.drain()
            VV.inc(vector.engine_nop())

        @block.scalar
        def _(scalar, h=h):
            VV.wait(scalar)
            scalar.activation(out=bnd[:], in_=bnd[:], func=AF.Ln)
            scalar.drain()
            scalar.activation(out=bnd[:], in_=bnd[:], func=AF.Exp, scale=0.5)
            scalar.drain()
            SS.inc(scalar.nop())

        @block.vector
        def _(vector, h=h):
            SS.wait(vector)
            vector.tensor_scalar(out=nmfull[:], in0=bnd[:], scalar1=-1.0,
                                 scalar2=None, op0=OP.mult)
            vector.drain()
            VV.inc(vector.engine_nop())
        nm_at = VV.n

        exp_ss = []
        for qtg in range(4):
            nkt = 4 * (qtg + 1)
            for ktp in range(nkt // 2):
                kts = [2 * ktp, 2 * ktp + 1]

                @block.tensor
                def _(tensor, h=h, qtg=qtg, ktp=ktp, kts=kts):
                    VV.wait(tensor, nm_at)
                    AUX.wait(tensor, tiles_at)
                    if len(exp_ss) >= 2:
                        SS.wait(tensor, exp_ss[-2])
                    ps = ps_sT[ktp % 2]
                    for sub, kt in enumerate(kts):
                        lo = 512 * sub
                        tensor.matmul(ps[:, lo:lo + 512],
                                      kh[h % 2][:, 128 * kt:128 * (kt + 1)],
                                      qh[h % 2][:, 512 * qtg:512 * (qtg + 1)],
                                      start=True, stop=False,
                                      skip_group_check=True)
                        tensor.matmul(ps[:, lo:lo + 512],
                                      tileB[h % 2][:, 128 * kt:128 * (kt + 1)],
                                      tileA[h % 2][:, 512 * qtg:512 * (qtg + 1)],
                                      start=False, stop=False,
                                      skip_group_check=True)
                        ins = tensor.matmul(ps[:, lo:lo + 512], onesb[0:1, 0:128],
                                            nmfull[0:1, 512 * qtg:512 * (qtg + 1)],
                                            start=False, stop=True,
                                            skip_group_check=True)
                    MM.inc(ins)

                diag = [kt for kt in kts if 0 <= kt - 4 * qtg < 4]
                if diag:
                    @block.vector
                    def _(vector, qtg=qtg, ktp=ktp, kts=kts, diag=diag):
                        MM.wait(vector)
                        MM.wait(vector, prev_av_at)
                        for kt in diag:
                            u = kt - 4 * qtg
                            sub = kt - kts[0]
                            vector.tensor_tensor(
                                out=ps_sT[ktp % 2][:, 512 * sub + 128 * u:
                                                   512 * sub + 128 * (u + 1)],
                                in0=ps_sT[ktp % 2][:, 512 * sub + 128 * u:
                                                    512 * sub + 128 * (u + 1)],
                                in1=mkq[:], op=OP.add)
                            if u > 0:
                                vector.memset(
                                    ps_sT[ktp % 2][:, 512 * sub:
                                                   512 * sub + 128 * u], NEG)
                        vector.drain()
                        VV.inc(vector.engine_nop())

                @block.scalar
                def _(scalar, qtg=qtg, ktp=ktp, kts=kts, diag=diag):
                    if diag:
                        VV.wait(scalar)
                    else:
                        MM.wait(scalar)
                    MM.wait(scalar, prev_av_at)
                    SS.inc(scalar.activation(
                        out=PT[:, 512 * kts[0]:512 * (kts[1] + 1)],
                        in_=ps_sT[ktp % 2][:], func=AF.Exp))
                exp_ss.append(SS.n)
            exp_at = SS.n
            mz_at = VV.n

            for u in range(4):
                qt = 4 * qtg + u

                @block.tensor
                def _(tensor, h=h, u=u, qt=qt, exp_at=exp_at, mz_at=mz_at):
                    SS.wait(tensor, exp_at)
                    VV.wait(tensor, mz_at)
                    LDC.wait(tensor, ldc_at)
                    FR.wait(tensor, vtok_at)
                    if len(avev) >= 2:
                        EV.wait(tensor, avev[-2])
                    pa = ps_av[u % 2]
                    for kt in range(qt + 1):
                        ins = tensor.matmul(
                            pa[:, 0:129],
                            PT[:, 512 * kt + 128 * u:512 * kt + 128 * (u + 1)],
                            vtok[:, 129 * kt:129 * (kt + 1)],
                            start=(kt == 0), stop=(kt == qt))
                    MM.inc(ins)

                @block.vector
                def _(vector, h=h, u=u, qt=qt):
                    MM.wait(vector)
                    vector.reciprocal(out=rinv[:], in_=ps_av[u % 2][:, 128:129])
                    vector.drain()
                    vector.tensor_scalar(out=att[qt][:, 128 * h:128 * (h + 1)],
                                         in0=ps_av[u % 2][:, 0:128],
                                         scalar1=rinv[:, :],
                                         scalar2=None, op0=OP.mult)
                    vector.drain()
                    EV.inc(vector.engine_nop())
                avev.append(EV.n)
            prev_av_at = MM.n
        head_mm_at.append(MM.n)

        @block.vector
        def _(vector):
            EV.wait(vector)
            vector.drain()
            PH.inc(vector.engine_nop())
        head_done_at.append(PH.n)

    phA3.close()

    phA4 = ExitStack()
    attT = phA4.enter_context(nc.sbuf_tensor("attT", [128, ND * S], BF16))
    ps_tpC = phA4.enter_context(nc.psum_tensor("ps_tpC", [128, 512], BF16))
    for t in range(NT):
        emit_transpose_chunks(
            att[t][:], ND,
            lambda j, t=t: attT[:, j * S + 128 * t:j * S + 128 * t + 128],
            ps_tpC, pre=[(PH, head_done_at[-1])])
    attT_at = FR.n
    st_att.close()

    x0t = [phA4.enter_context(nc.sbuf_tensor(f"x0t{u}", [128, D], F32))
           for u in range(4)]
    wob = [phA4.enter_context(nc.sbuf_tensor(f"wob{i}", [128, ND * 128], BF16))
           for i in range(2)]
    stgo = [phA4.enter_context(nc.sbuf_tensor(f"stgo{i}", [128, 512], BF16))
            for i in range(2)]
    xpc = [phA4.enter_context(nc.sbuf_tensor(f"xpc{u}", [128, 128], F32))
           for u in range(4)]
    ps_o = [phA4.enter_context(nc.psum_tensor(f"ps_o{i}", [128, 512], F32))
            for i in range(2)]
    ps_ot = phA4.enter_context(nc.psum_tensor("ps_ot", [128, 512], BF16))

    slab_used = []
    res_done_at = []
    it = 0
    for tb in range(4):
        @block.sync
        def _(sync, tb=tb):
            if tb > 0:
                VV.wait(sync, res_done_at[-1])
            else:
                PH.wait(sync, head_done_at[-1])
            for u in range(4):
                t = 4 * tb + u
                LDX.incd(sync.dma_start(out=x0t[u][:],
                                        in_=x0_d[128 * t:128 * (t + 1), :]))

        for do in range(ND):
            @block.gpsimd
            def _(gpsimd, tb=tb, do=do, it=it):
                if len(slab_used) >= 2:
                    MM.wait(gpsimd, slab_used[-2])
                else:
                    PH.wait(gpsimd, head_done_at[-1])
                wv_ = wo_d.rearrange("(dc dp) o -> dp dc o", dp=128)
                LDWP.incd(gpsimd.dma_start(
                    out=wob[it % 2][:].rearrange("p (dc o) -> p dc o", dc=ND),
                    in_=wv_[:, :, 128 * do:128 * (do + 1)]), it)

            @block.tensor
            def _(tensor, tb=tb, do=do, it=it):
                LDWP.wait(tensor, it)
                LDC.wait(tensor, ldc_at)
                LDCG.wait(tensor, brow_at)
                FR.wait(tensor, attT_at)
                if it == 0:
                    EV.wait(tensor)
                elif EV.n >= 1:
                    EV.wait(tensor, EV.n - 1)
                ps = ps_o[it % 2]
                for d in range(ND):
                    tensor.matmul(ps[:], wob[it % 2][:, 128 * d:128 * (d + 1)],
                                  attT[:, d * S + 512 * tb:d * S + 512 * (tb + 1)],
                                  start=(d == 0), stop=False)
                MM.inc(tensor.matmul(ps[:],
                                     brow["bo"][0:1, 128 * do:128 * (do + 1)],
                                     onesb[0:1, 0:512], start=False, stop=True))
            slab_used.append(MM.n)

            @block.vector
            def _(vector, tb=tb, do=do):
                MM.wait(vector)
                TT.wait(vector)
                vector.tensor_copy(out=stgo[it % 2][:], in_=ps_o[it % 2][:])
                vector.drain()
                EV.inc(vector.engine_nop())

            @block.tensor
            def _(tensor, tb=tb, do=do):
                EV.wait(tensor)
                VV.wait(tensor)
                for u in range(4):
                    ins = tensor.transpose(ps_ot[:, 128 * u:128 * (u + 1)],
                                           stgo[it % 2][:, 128 * u:128 * (u + 1)],
                                           ident[:])
                TT.inc(ins)

            @block.vector
            def _(vector, tb=tb, do=do):
                TT.wait(vector)
                LDX.wait(vector)
                STG.wait(vector)
                for u in range(4):
                    vector.scalar_tensor_tensor(
                        out=xpc[u][:], in0=ps_ot[:, 128 * u:128 * (u + 1)],
                        scalar=1.0, in1=x0t[u][:, 128 * do:128 * (do + 1)],
                        op0=OP.mult, op1=OP.add)
                vector.drain()
                VV.inc(vector.engine_nop())

            @block.sync
            def _(sync, tb=tb, do=do):
                VV.wait(sync)
                for u in range(4):
                    t = 4 * tb + u
                    STG.incd(sync.dma_start(
                        out=x1_s[128 * t:128 * (t + 1), 128 * do:128 * (do + 1)],
                        in_=xpc[u][:]))
            it += 1
        res_done_at.append(VV.n)
    phA4.close()
    phA.close()

    if upto >= 2:
        phBC = ExitStack()
        sbB = lambda nm, shp, dt: phBC.enter_context(nc.sbuf_tensor(nm, shp, dt))
        xn2T = [sbB(f"xn2T{i}", [128, ND * 512], BF16) for i in range(2)]
        fuse = sbB("fuse", [128, NI0 * 512], BF16)
        g8 = [sbB(f"g8{i}", [128, ND * 128], F8) for i in range(2)]
        u8 = [sbB(f"u8{i}", [128, ND * 128], F8) for i in range(2)]
        d8 = [sbB(f"d8{i}", [128, NI0 * 128], F8) for i in range(2)]
        gact = sbB("gact", [128, 512], BF16)
        x1t = [sbB("x1t0", [128, D], F32)]
        x1t.append(x1t[0])
        xn2st = sbB("xn2st", [128, D], BF16)
        x1rs = [sbB(f"x1rs{u}", [128, 128], F32) for u in range(4)]
        stgd = [sbB(f"stgd{i}", [128, 512], BF16) for i in range(2)]
        xpcF = [sbB(f"xpcF{u}", [128, 128], F32) for u in range(4)]
        if upto >= 3:
            w2b = sbB("w2b", [128, 16 * 384], BF16)
            bqkv2r = sbB("bqkv2r", [1, 16 * 384], BF16)
            wobd = sbB("wobd", [128, 128], BF16)
            bo2r = sbB("bo2r", [1, 128], BF16)
            x2t = [sbB("x2t0", [128, D], F32)]
            x2t.append(x2t[0])
            xn3 = sbB("xn3", [128, D], BF16)
            xn3T = xn3
            qkv = sbB("qkv", [128, 3 * D], BF16)
            sct = [sbB(f"sct{i}", [128, 2048], F32) for i in range(2)]
            p0 = [sbB(f"p0{i}", [128, 2048], BF16) for i in range(2)]
            zz = sbB("zz", [128, 64], F32)
            rz = sbB("rz", [128, 64], F32)
            v2 = sbB("v2", [128, 64], BF16)
            ctxa = sbB("ctxa", [128, D], BF16)
            ctxT = sbB("ctxT", [128, D], BF16)
            x3p = [sbB("x3p0", [128, 512], F32)]
            x3p.append(x3p[0])
        ps_tpF = phBC.enter_context(nc.psum_tensor("ps_tpF", [128, 512], BF16))
        ps_gB = [phBC.enter_context(nc.psum_tensor(f"ps_gB{i}", [128, 512], F32))
                 for i in range(2)]
        ps_uB = [phBC.enter_context(nc.psum_tensor(f"ps_uB{i}", [128, 512], F32))
                 for i in range(2)]
        ps_sh = [phBC.enter_context(nc.psum_tensor(f"ps_sh{i}", [128, 512], F32))
                 for i in range(2)]
        ps_otF = phBC.enter_context(nc.psum_tensor("ps_otF", [128, 512], BF16))
        sh_ev = [Ctr(nc, ctx, f"shev{i}") for i in range(2)]
        LDXC = PCtr(nc, ctx, "ldxc", 2)
        STC = Ctr(nc, ctx, "stc")

        if upto >= 3:
            @block.gpsimd
            def _(gpsimd):
                MM.wait(gpsimd)
                VV.wait(gpsimd)
                SS.wait(gpsimd)
                STG.wait(gpsimd)
                LDCG.incd(gpsimd.dma_start(
                    out=w2b[:].rearrange("p (c o) -> p c o", c=16),
                    in_=w2_d.transpose([1, 0, 2])))
                LDCG.incd(gpsimd.dma_start(out=bqkv2r[:], in_=bqkv2_d))
                LDCG.incd(gpsimd.dma_start(out=wobd[:], in_=woutbd_d))
                LDCG.incd(gpsimd.dma_start(out=bo2r[:], in_=bout2_d))
            w2_at = LDCG.n

        b_x2_at = []
        c_state = {"resid_at": [], "lt": -1}

        def c_chunk_gen():
            if upto < 3:
                return
            for lt in range(NT):
                yield ("x2", lt // 4)
                @block.sync
                def _(sync, lt=lt):
                    STG.wait(sync, b_x2_at[lt // 4])
                    if c_state["resid_at"]:
                        VV.wait(sync, c_state["resid_at"][-1])
                    LDXC.incd(sync.dma_start(
                        out=x2t[lt % 2][:],
                        in_=x2_s[128 * lt:128 * (lt + 1), :]), lt)

                emit_ln_tile(x2t[lt % 2][:], xn3[:],
                             pre_waits_v=[(LDXC.c[lt % 2], LDXC.c[lt % 2].n)],
                             pre_waits_s=[(MM, MM.n)] if lt > 0 else [])
                yield None
                emit_transpose_chunks(xn3[:], ND,
                                      lambda j: xn3T[:, 128 * j:128 * (j + 1)],
                                      ps_tpF, pre=[(SS, SS.n)])
                xn3T_at = FR.n
                yield None
                for pr in range(16):
                    bk = pr % 2

                    @block.tensor
                    def _(tensor, lt=lt, pr=pr, bk=bk, xn3T_at=xn3T_at):
                        FR.wait(tensor, xn3T_at)
                        LDCG.wait(tensor, w2_at)
                        LDC.wait(tensor, ldc_at)
                        sh_ev[bk].wait(tensor)
                        ps = ps_sh[bk]
                        tensor.matmul(ps[:, 0:384],
                                      xn3T[:, 128 * pr:128 * (pr + 1)],
                                      w2b[:, 384 * pr:384 * (pr + 1)],
                                      start=True, stop=False)
                        MM.inc(tensor.matmul(ps[:, 0:384], onesb[0:1, 0:128],
                                             bqkv2r[0:1, 384 * pr:384 * (pr + 1)],
                                             start=False, stop=True))

                    @block.vector
                    def _(vector, pr=pr, bk=bk):
                        MM.wait(vector)
                        vector.tensor_copy(out=qkv[:, 384 * pr:384 * (pr + 1)],
                                           in_=ps_sh[bk][:, 0:384])
                        vector.drain()
                        sh_ev[bk].inc(vector.engine_nop())
                    if pr % 4 == 3:
                        yield None
                qkv_at = [sh_ev[0].n, sh_ev[1].n]

                qv_ = qkv[:].rearrange("p (c t d) -> p c t d", c=32, t=3)
                mult_vv = {}
                expd_ss = {}
                rest_vv = {-2: 0, -1: 0}

                def emit_mult(dcg, lt=lt, qkv_at=qkv_at):
                    @block.vector
                    def _(vector, dcg=dcg):
                        sh_ev[0].wait(vector, qkv_at[0])
                        sh_ev[1].wait(vector, qkv_at[1])
                        VV.wait(vector, rest_vv[dcg - 2])
                        qs = qv_[:, :, 0, 2 * dcg:2 * dcg + 2]
                        ks = qv_[:, :, 1, 2 * dcg:2 * dcg + 2]
                        qb = qs.unsqueeze(2).broadcast_to([128, 32, 32, 2])
                        kb = ks.unsqueeze(1).broadcast_to([128, 32, 32, 2])
                        sv = sct[dcg % 2][:].rearrange("p (c e d) -> p c e d",
                                                       c=32, e=32)
                        vector.tensor_tensor(out=sv, in0=qb, in1=kb, op=OP.mult)
                        vector.drain()
                        VV.inc(vector.engine_nop())
                    mult_vv[dcg] = VV.n

                def emit_exp(dcg):
                    @block.scalar
                    def _(scalar, dcg=dcg):
                        VV.wait(scalar, mult_vv[dcg])
                        VV.wait(scalar, rest_vv[dcg - 2])
                        SS.inc(scalar.activation(out=p0[dcg % 2][:],
                                                 in_=sct[dcg % 2][:],
                                                 func=AF.Exp, scale=0.125))
                    expd_ss[dcg] = SS.n

                def emit_rest(dcg, lt=lt):
                    @block.vector
                    def _(vector, dcg=dcg):
                        SS.wait(vector, expd_ss[dcg])
                        p0v = p0[dcg % 2][:].rearrange("p (c e d) -> p c e d",
                                                       c=32, e=32)
                        zv = zz[:].rearrange("p (c d) -> p c d", c=32)
                        vector.tensor_reduce(out=zv,
                                             in_=p0v.transpose([0, 1, 3, 2]),
                                             axis=AX.X, op=OP.add)
                        vector.drain()
                        rzv = rz[:].rearrange("p (c d) -> p c d", c=32)
                        vector.reciprocal(out=rzv, in_=zv)
                        vector.drain()
                        vslice = qv_[:, :, 2, 2 * dcg:2 * dcg + 2]
                        v2v = v2[:].rearrange("p (c d) -> p c d", c=32)
                        vector.tensor_tensor(out=v2v, in0=vslice, in1=rzv,
                                             op=OP.mult)
                        vector.drain()
                        v2b = v2[:].rearrange("p (c d) -> p c d", c=32)
                        v2b = v2b.unsqueeze(2).broadcast_to([128, 32, 32, 2])
                        prv = sct[dcg % 2][:].bitcast(BF16)[:, 0:2048].rearrange(
                            "p (c e d) -> p c e d", c=32, e=32)
                        vector.tensor_tensor(out=prv, in0=p0v, in1=v2b,
                                             op=OP.mult)
                        vector.drain()
                        cvw = ctxa[:].rearrange("p (m d) -> p m d", m=32)[:, :, 2 * dcg:2 * dcg + 2]
                        with nc.allow_low_precision(reason="fp32-accumulated reduce to bf16 matmul operand"):
                            vector.tensor_reduce(out=cvw,
                                                 in_=prv.transpose([0, 2, 3, 1]),
                                                 axis=AX.X, op=OP.add)
                        vector.drain()
                        VV.inc(vector.engine_nop())
                    rest_vv[dcg] = VV.n

                for dcg in range(32):
                    emit_mult(dcg)
                    emit_exp(dcg)
                    if dcg >= 1:
                        emit_rest(dcg - 1)
                    yield None
                emit_rest(31)

                emit_transpose_chunks(ctxa[:], ND,
                                      lambda j: ctxT[:, 128 * j:128 * (j + 1)],
                                      ps_tpF, pre=[(VV, VV.n)])
                ctxT_at = FR.n
                yield None
                for bkr in range(4):
                    bk = bkr % 2

                    @block.tensor
                    def _(tensor, lt=lt, bkr=bkr, bk=bk, ctxT_at=ctxT_at):
                        FR.wait(tensor, ctxT_at)
                        sh_ev[bk].wait(tensor)
                        for pr in range(4):
                            j = 4 * bkr + pr
                            tensor.matmul(ps_sh[bk][:, 128 * pr:128 * (pr + 1)],
                                          ctxT[:, 128 * j:128 * (j + 1)], wobd[:],
                                          start=True, stop=False,
                                          skip_group_check=True)
                            ins = tensor.matmul(
                                ps_sh[bk][:, 128 * pr:128 * (pr + 1)],
                                onesb[0:1, 0:128], bo2r[0:1, :],
                                start=False, stop=True, skip_group_check=True)
                        MM.inc(ins)

                    @block.vector
                    def _(vector, lt=lt, bkr=bkr, bk=bk):
                        MM.wait(vector)
                        STC.wait(vector)
                        vector.scalar_tensor_tensor(
                            out=x3p[bkr % 2][:], in0=ps_sh[bk][:],
                            scalar=1.0,
                            in1=x2t[lt % 2][:, 512 * bkr:512 * (bkr + 1)],
                            op0=OP.mult, op1=OP.add)
                        vector.drain()
                        sh_ev[bk].inc(vector.engine_nop())
                        VV.inc(vector.engine_nop())

                    @block.sync
                    def _(sync, lt=lt, bkr=bkr):
                        VV.wait(sync)
                        STC.incd(sync.dma_start(
                            out=x3_s[128 * lt:128 * (lt + 1),
                                     512 * bkr:512 * (bkr + 1)],
                            in_=x3p[bkr % 2][:]))
                c_state["resid_at"].append(VV.n)
                yield None

        cgen = c_chunk_gen()
        cgen_pending = [next(cgen, "DONE")] if upto >= 3 else ["DONE"]

        def drive_c(k=1):
            for _ in range(k):
                p = cgen_pending[0]
                if p == "DONE":
                    return
                if isinstance(p, tuple) and p[0] == "x2":
                    if len(b_x2_at) <= p[1]:
                        return
                cgen_pending[0] = next(cgen, "DONE")

        ln_ss_at = []
        b1_tp_at = {}

        def emit_b1(tb):
            for u in range(4):
                t = 4 * tb + u

                @block.sync
                def _(sync, t=t, tb=tb):
                    if t >= 1:
                        SS.wait(sync, ln_ss_at[t - 1])
                    if t < 4:
                        STG.wait(sync, STG.n)
                    LDXP.incd(sync.dma_start(
                        out=x1t[t % 2][:],
                        in_=x1_s[128 * t:128 * (t + 1), :]), t)

                pre_v = [(LDXP.c[t % 2], LDXP.c[t % 2].n)]
                pre_s = [(MM, MM.n)] if t > 0 else []
                emit_ln_tile(x1t[t % 2][:], xn2st[:], pre_waits_v=pre_v,
                             pre_waits_s=pre_s)
                ln_ss_at.append(SS.n)
                pre_t = [(SS, SS.n)]
                if tb == 0 and u == 0:
                    pre_t += [(VV, VV.n), (EV, EV.n)]
                if tb >= 2:
                    pre_t += [(MM, gate_mm_at.get(tb - 2, 0))]
                emit_transpose_chunks(
                    xn2st[:], ND,
                    lambda j, u=u, tb=tb: xn2T[tb % 2][:, 512 * j + 128 * u:
                                                       512 * j + 128 * (u + 1)],
                    ps_tpF, pre=pre_t)
            b1_tp_at[tb] = FR.n

        gate_mm_at = {}
        gslab_used = [0, 0]
        dslab_used = [0, 0]
        fuse_vv = []
        dn_ev = []
        emit_b1(0)
        for tb in range(4):
            if tb < 3:
                emit_b1(tb + 1)
            fuse_done = []
            for i in range(NI0):
                @block.gpsimd
                def _(gpsimd, tb=tb, i=i):
                    if gslab_used[i % 2]:
                        MM.wait(gpsimd, gslab_used[i % 2])
                    else:
                        SS.wait(gpsimd)
                        STG.wait(gpsimd)
                        FR.wait(gpsimd, b1_tp_at[0])
                    gv = gw_d.rearrange("(dc dp) o -> dp dc o", dp=128)
                    uv = uw_d.rearrange("(dc dp) o -> dp dc o", dp=128)
                    LDWG.incd(gpsimd.dma_start(
                        out=g8[i % 2][:].rearrange("p (dc o) -> p dc o", dc=ND),
                        in_=gv[:, :, 128 * i:128 * (i + 1)]), i)
                    LDWG.incd(gpsimd.dma_start(
                        out=u8[i % 2][:].rearrange("p (dc o) -> p dc o", dc=ND),
                        in_=uv[:, :, 128 * i:128 * (i + 1)]), i)

                @block.tensor
                def _(tensor, tb=tb, i=i):
                    LDWG.wait(tensor, i)
                    FR.wait(tensor, b1_tp_at[tb])
                    if len(fuse_done) >= 2:
                        VV.wait(tensor, fuse_done[-2])
                    elif fuse_vv:
                        VV.wait(tensor, fuse_vv[-1])
                    for d in range(ND):
                        tensor.matmul(ps_gB[i % 2][:],
                                      g8[i % 2][:, 128 * d:128 * (d + 1)],
                                      xn2T[tb % 2][:, 512 * d:512 * (d + 1)],
                                      start=(d == 0), stop=(d == ND - 1),
                                      skip_group_check=True)
                    for d in range(ND):
                        ins = tensor.matmul(
                            ps_uB[i % 2][:], u8[i % 2][:, 128 * d:128 * (d + 1)],
                            xn2T[tb % 2][:, 512 * d:512 * (d + 1)],
                            start=(d == 0), stop=(d == ND - 1),
                            skip_group_check=True)
                    MM.inc(ins)
                gslab_used[i % 2] = MM.n

                @block.scalar
                def _(scalar, tb=tb, i=i):
                    MM.wait(scalar)
                    if fuse_done:
                        VV.wait(scalar, fuse_done[-1])
                    elif fuse_vv:
                        VV.wait(scalar, fuse_vv[-1])
                    SS.inc(scalar.activation(out=gact[:], in_=ps_gB[i % 2][:],
                                             func=AF.Silu,
                                             scale=gs_sb[:, i:i + 1]))

                @block.vector
                def _(vector, tb=tb, i=i):
                    SS.wait(vector)
                    vector.scalar_tensor_tensor(
                        out=fuse[:, 512 * i:512 * (i + 1)],
                        in0=ps_uB[i % 2][:], scalar=us_sb[:, i:i + 1],
                        in1=gact[:], op0=OP.mult, op1=OP.mult)
                    vector.drain()
                    VV.inc(vector.engine_nop())
                fuse_done.append(VV.n)
                drive_c(6)
            fuse_vv.append(fuse_done[-1])
            gate_mm_at[tb] = MM.n

            for do in range(ND):
                @block.gpsimd
                def _(gpsimd, tb=tb, do=do):
                    if dslab_used[do % 2]:
                        MM.wait(gpsimd, dslab_used[do % 2])
                    else:
                        SS.wait(gpsimd)
                        STG.wait(gpsimd)
                        FR.wait(gpsimd, b1_tp_at[0])
                    dv = dw_d.rearrange("(ic ip) o -> ip ic o", ip=128)
                    LDWB.incd(gpsimd.dma_start(
                        out=d8[do % 2][:].rearrange("p (ic o) -> p ic o", ic=NI0),
                        in_=dv[:, :, 128 * do:128 * (do + 1)]), do)

                @block.sync
                def _(sync, tb=tb, do=do):
                    if dn_ev:
                        VV.wait(sync, dn_ev[-1])
                    else:
                        STG.wait(sync, STG.n)
                    for u in range(4):
                        t = 4 * tb + u
                        LDX.incd(sync.dma_start(
                            out=x1rs[u][:],
                            in_=x1_s[128 * t:128 * (t + 1),
                                     128 * do:128 * (do + 1)]))

                @block.tensor
                def _(tensor, tb=tb, do=do):
                    LDWB.wait(tensor, do)
                    VV.wait(tensor, fuse_done[-1])
                    sh_ev[do % 2].wait(tensor)
                    ps = ps_sh[do % 2]
                    for ic in range(NI0):
                        ins = tensor.matmul(
                            ps[:], d8[do % 2][:, 128 * ic:128 * (ic + 1)],
                            fuse[:, 512 * ic:512 * (ic + 1)],
                            start=(ic == 0), stop=(ic == NI0 - 1))
                    MM.inc(ins)

                @block.vector
                def _(vector, tb=tb, do=do):
                    MM.wait(vector)
                    TT.wait(vector)
                    vector.tensor_scalar(out=stgd[do % 2][:],
                                         in0=ps_sh[do % 2][:],
                                         scalar1=ds_sb[:, do:do + 1],
                                         scalar2=None, op0=OP.mult)
                    vector.drain()
                    EV.inc(vector.engine_nop())
                    sh_ev[do % 2].inc(vector.engine_nop())

                @block.tensor
                def _(tensor, tb=tb, do=do):
                    EV.wait(tensor)
                    if dn_ev:
                        VV.wait(tensor, dn_ev[-1])
                    else:
                        VV.wait(tensor)
                    for u in range(4):
                        ins = tensor.transpose(ps_otF[:, 128 * u:128 * (u + 1)],
                                               stgd[do % 2][:, 128 * u:128 * (u + 1)],
                                               ident[:])
                    TT.inc(ins)

                @block.vector
                def _(vector, tb=tb, do=do):
                    TT.wait(vector)
                    LDX.wait(vector)
                    STG.wait(vector)
                    for u in range(4):
                        vector.scalar_tensor_tensor(
                            out=xpcF[u][:], in0=ps_otF[:, 128 * u:128 * (u + 1)],
                            scalar=1.0, in1=x1rs[u][:],
                            op0=OP.mult, op1=OP.add)
                    vector.drain()
                    VV.inc(vector.engine_nop())
                dn_ev.append(VV.n)

                @block.sync
                def _(sync, tb=tb, do=do):
                    VV.wait(sync)
                    for u in range(4):
                        t = 4 * tb + u
                        STG.incd(sync.dma_start(
                            out=x2_s[128 * t:128 * (t + 1),
                                     128 * do:128 * (do + 1)],
                            in_=xpcF[u][:]))
                drive_c(6)
            b_x2_at.append(STG.n)

        while cgen_pending[0] != "DONE":
            p = cgen_pending[0]
            cgen_pending[0] = next(cgen, "DONE")
        phBC.close()

    if upto >= 4:
        phD = ExitStack()
        sbD = lambda nm, shp, dt: phD.enter_context(nc.sbuf_tensor(nm, shp, dt))
        xn4T = sbD("xn4T", [128, ND * S], BF16)
        fuseM = sbD("fuseM", [128, NIM * S], BF16)
        routT = sbD("routT", [128, ND * 8], BF16)
        lgall = sbD("lgall", [128, NT * 8], F32)
        top8 = sbD("top8", [128, 8], F32)
        wgt = sbD("wgt", [128, 1], F32)
        cmb = sbD("cmb", [128, NT * 8], F32)
        rb16 = sbD("rb16", [8, D], BF16)

        c_st_at = STC.n
        phD1 = ExitStack()
        x3t = [phD1.enter_context(nc.sbuf_tensor(f"x3t{i}", [128, D], F32))
               for i in range(2)]
        xn4 = phD1.enter_context(nc.sbuf_tensor("xn4", [128, D], BF16))
        ps_tpD = phD1.enter_context(nc.psum_tensor("ps_tpD", [128, 512], BF16))
        ps_lg = phD1.enter_context(nc.psum_tensor("ps_lg", [128, 8], F32))

        @block.gpsimd
        def _(gpsimd):
            MM.wait(gpsimd)
            VV.wait(gpsimd)
            SS.wait(gpsimd)
            STG.wait(gpsimd)
            STC.wait(gpsimd)
            LDCG.incd(gpsimd.dma_start(out=rb16[:], in_=rw_d))
        rw_at = LDCG.n
        for j in range(ND):
            @block.tensor
            def _(tensor, j=j):
                LDCG.wait(tensor, rw_at)
                FR.wait(tensor)
                MM.inc(tensor.transpose(ps_tpD[:, 0:8],
                                        rb16[:, 128 * j:128 * (j + 1)],
                                        ident[0:8, 0:8]))

            @block.vector
            def _(vector, j=j):
                MM.wait(vector)
                vector.tensor_copy(out=routT[:, 8 * j:8 * (j + 1)],
                                   in_=ps_tpD[:, 0:8])
                vector.drain()
                FR.inc(vector.engine_nop())
        routT_at = FR.n

        xn_done_at = []
        for t in range(NT):
            @block.sync
            def _(sync, t=t):
                STC.wait(sync, c_st_at)
                if t >= 2:
                    SS.wait(sync, xn_done_at[t - 2])
                LDXP.incd(sync.dma_start(out=x3t[t % 2][:],
                                         in_=x3_s[128 * t:128 * (t + 1), :]), t)

            pw_s = [(MM, MM.n)] if t > 0 else []
            emit_ln_tile(x3t[t % 2][:], xn4[:],
                         pre_waits_v=[(LDXP.c[t % 2], LDXP.c[t % 2].n)],
                         pre_waits_s=pw_s)
            xn_done_at.append(SS.n)
            pre_t = [(SS, SS.n)] + ([(VV, VV.n), (EV, EV.n)] if t == 0 else [])
            emit_transpose_chunks(
                xn4[:], ND,
                lambda j, t=t: xn4T[:, j * S + 128 * t:j * S + 128 * t + 128],
                ps_tpD, pre=pre_t)

            @block.tensor
            def _(tensor, t=t):
                FR.wait(tensor, routT_at)
                FR.wait(tensor)
                LG.wait(tensor)
                for d in range(ND):
                    ins = tensor.matmul(
                        ps_lg[:], xn4T[:, d * S + 128 * t:d * S + 128 * (t + 1)],
                        routT[:, 8 * d:8 * (d + 1)],
                        start=(d == 0), stop=(d == ND - 1))
                MM.inc(ins)

            @block.vector
            def _(vector, t=t):
                MM.wait(vector)
                vector.tensor_copy(out=lgall[:, 8 * t:8 * (t + 1)], in_=ps_lg[:])
                vector.drain()
                vector.max(out=top8[:], in_=lgall[:, 8 * t:8 * (t + 1)])
                vector.drain()
                vector.tensor_tensor(out=wgt[:], in0=top8[:, 1:2],
                                     in1=top8[:, 0:1], op=OP.subtract)
                vector.drain()
                vector.drain()
                LG.inc(vector.engine_nop())

            @block.scalar
            def _(scalar, t=t):
                LG.wait(scalar)
                scalar.activation(out=wgt[:], in_=wgt[:], func=AF.Exp)
                scalar.drain()
                SS.inc(scalar.nop())

            @block.vector
            def _(vector, t=t):
                SS.wait(vector)
                vector.tensor_scalar(out=wgt[:], in0=wgt[:], scalar1=1.0,
                                     scalar2=None, op0=OP.add)
                vector.drain()
                vector.reciprocal(out=wgt[:], in_=wgt[:])
                vector.drain()
                vector.tensor_scalar(out=cmb[:, 8 * t:8 * (t + 1)],
                                     in0=lgall[:, 8 * t:8 * (t + 1)],
                                     scalar1=top8[:, 0:1], scalar2=wgt[:, 0:1],
                                     op0=OP.is_equal, op1=OP.mult)
                vector.drain()
                vector.drain()
                LG.inc(vector.engine_nop())
        routing_at = LG.n
        phD1.close()

        phE = ExitStack()
        gwb = [phE.enter_context(nc.sbuf_tensor(f"gwb{i}", [128, ND * 128], BF16))
               for i in range(2)]
        uwb = [phE.enter_context(nc.sbuf_tensor(f"uwb{i}", [128, ND * 128], BF16))
               for i in range(2)]
        gactM = phE.enter_context(nc.sbuf_tensor("gactM", [128, 512], BF16))
        dwb = [phE.enter_context(nc.sbuf_tensor(f"dwb{i}", [128, NIM * 128], BF16))
               for i in range(3)]
        stgm = [phE.enter_context(nc.sbuf_tensor(f"stgm{i}", [128, 512], BF16))
                for i in range(2)]
        x3strip = phE.enter_context(nc.sbuf_tensor("x3strip", [128, NT * 128], F32))
        ostrip = phE.enter_context(nc.sbuf_tensor("ostrip", [128, NT * 128], F32))
        ps_gM = [phE.enter_context(nc.psum_tensor(f"ps_gM{i}", [128, 512], F32))
                 for i in range(2)]
        ps_uM = [phE.enter_context(nc.psum_tensor(f"ps_uM{i}", [128, 512], F32))
                 for i in range(2)]
        ps_dM = [phE.enter_context(nc.psum_tensor(f"ps_dM{i}", [128, 512], F32))
                 for i in range(2)]
        ps_otM = phE.enter_context(nc.psum_tensor("ps_otM", [128, 512], BF16))

        slab_used = []
        dslab_used = {}
        fuse_done = []
        fuse_tb = {}
        dn_mm_prev = 0
        prev_store_at = ACC.n
        acc_prev = None
        strip_vv = []
        for e in range(NE):
            for i in range(NIM):
                @block.gpsimd
                def _(gpsimd, e=e, i=i):
                    if len(slab_used) >= 2:
                        MM.wait(gpsimd, slab_used[-2])
                    else:
                        MM.wait(gpsimd)
                        VV.wait(gpsimd)
                        SS.wait(gpsimd)
                        STG.wait(gpsimd)
                        STC.wait(gpsimd)
                    gv = mgw_d[e].rearrange("(hc hp) o -> hp hc o", hp=128)
                    uv = muw_d[e].rearrange("(hc hp) o -> hp hc o", hp=128)
                    LDWM.incd(gpsimd.dma_start(
                        out=gwb[i % 2][:].rearrange("p (hc o) -> p hc o", hc=ND),
                        in_=gv[:, :, 128 * i:128 * (i + 1)]), i)
                    LDWM.incd(gpsimd.dma_start(
                        out=uwb[i % 2][:].rearrange("p (hc o) -> p hc o", hc=ND),
                        in_=uv[:, :, 128 * i:128 * (i + 1)]), i)

                for tb in range(4):
                    bk = tb % 2

                    @block.tensor
                    def _(tensor, e=e, i=i, tb=tb, bk=bk):
                        LDWM.wait(tensor, i)
                        LG.wait(tensor, routing_at)
                        if fuse_tb.get(bk):
                            VV.wait(tensor, fuse_tb[bk][-1])
                        for d in range(ND):
                            tensor.matmul(ps_gM[bk][:],
                                          gwb[i % 2][:, 128 * d:128 * (d + 1)],
                                          xn4T[:, d * S + 512 * tb:d * S + 512 * (tb + 1)],
                                          start=(d == 0), stop=(d == ND - 1),
                                          skip_group_check=True)
                        for d in range(ND):
                            ins = tensor.matmul(
                                ps_uM[bk][:],
                                uwb[i % 2][:, 128 * d:128 * (d + 1)],
                                xn4T[:, d * S + 512 * tb:d * S + 512 * (tb + 1)],
                                start=(d == 0), stop=(d == ND - 1),
                                skip_group_check=True)
                        MM.inc(ins)

                    @block.scalar
                    def _(scalar, i=i, tb=tb, bk=bk):
                        MM.wait(scalar)
                        if fuse_done:
                            VV.wait(scalar, fuse_done[-1])
                        SS.inc(scalar.activation(out=gactM[:], in_=ps_gM[bk][:],
                                                 func=AF.Silu))

                    @block.vector
                    def _(vector, e=e, i=i, tb=tb, bk=bk):
                        SS.wait(vector)
                        MM.wait(vector, dn_mm_prev)
                        vector.scalar_tensor_tensor(
                            out=fuseM[:, i * S + 512 * tb:i * S + 512 * (tb + 1)],
                            in0=ps_uM[bk][:], scalar=1.0, in1=gactM[:],
                            op0=OP.mult, op1=OP.mult)
                        vector.drain()
                        VV.inc(vector.engine_nop())
                    fuse_tb.setdefault(bk, []).append(VV.n)
                    fuse_done.append(VV.n)
                slab_used.append(MM.n)
            gate_done_vv = VV.n

            def emit_dwb_load(hc, e=e):
                @block.gpsimd
                def _(gpsimd, e=e, hc=hc):
                    if dslab_used.get(hc % 3):
                        MM.wait(gpsimd, dslab_used[hc % 3])
                    else:
                        MM.wait(gpsimd)
                        VV.wait(gpsimd)
                        STC.wait(gpsimd)
                    dv = mdw_d[e].rearrange("(ic ip) o -> ip ic o", ip=128)
                    LDWD.incd(gpsimd.dma_start(
                        out=dwb[hc % 3][:].rearrange("p (ic o) -> p ic o", ic=NIM),
                        in_=dv[:, :, 128 * hc:128 * (hc + 1)]), hc)

            it = 0
            emit_dwb_load(0)
            emit_dwb_load(1)
            for hc in range(ND):
                if hc + 2 < ND:
                    emit_dwb_load(hc + 2)

                if e == 0:
                    @block.sync
                    def _(sync, hc=hc):
                        if strip_vv:
                            VV.wait(sync, strip_vv[-1])
                        x3v = x3_s.rearrange("(tt tp) h -> tp tt h", tp=128)
                        LDX.incd(sync.dma_start(
                            out=x3strip[:].rearrange("p (tt h) -> p tt h", tt=NT),
                            in_=x3v[:, :, 128 * hc:128 * (hc + 1)]))

                for tb in range(4):
                    @block.tensor
                    def _(tensor, e=e, hc=hc, tb=tb, it=it):
                        LDWD.wait(tensor, hc)
                        VV.wait(tensor, fuse_done[-1])
                        if EV.n >= 1:
                            EV.wait(tensor, EV.n - 1)
                        ps = ps_dM[it % 2]
                        for ic in range(NIM):
                            ins = tensor.matmul(
                                ps[:], dwb[hc % 3][:, 128 * ic:128 * (ic + 1)],
                                fuseM[:, ic * S + 512 * tb:ic * S + 512 * (tb + 1)],
                                start=(ic == 0), stop=(ic == NIM - 1))
                        MM.inc(ins)

                    @block.vector
                    def _(vector, hc=hc, tb=tb, it=it):
                        MM.wait(vector)
                        TT.wait(vector)
                        vector.tensor_copy(out=stgm[it % 2][:],
                                           in_=ps_dM[it % 2][:])
                        vector.drain()
                        EV.inc(vector.engine_nop())

                    @block.tensor
                    def _(tensor, hc=hc, tb=tb, it=it):
                        EV.wait(tensor)
                        if strip_vv:
                            VV.wait(tensor, strip_vv[-1])
                        for u in range(4):
                            ins = tensor.transpose(
                                ps_otM[:, 128 * u:128 * (u + 1)],
                                stgm[it % 2][:, 128 * u:128 * (u + 1)],
                                ident[:])
                        TT.inc(ins)

                    @block.vector
                    def _(vector, e=e, hc=hc, tb=tb):
                        TT.wait(vector)
                        if e == 0:
                            LDX.wait(vector)
                        ACC.wait(vector, prev_store_at)
                        for u in range(4):
                            t = 4 * tb + u
                            csc = cmb[:, 8 * t + e:8 * t + e + 1]
                            if e == 0:
                                vector.scalar_tensor_tensor(
                                    out=ostrip[:, 128 * t:128 * (t + 1)],
                                    in0=ps_otM[:, 128 * u:128 * (u + 1)],
                                    scalar=csc,
                                    in1=x3strip[:, 128 * t:128 * (t + 1)],
                                    op0=OP.mult, op1=OP.add)
                            else:
                                vector.tensor_scalar(
                                    out=ostrip[:, 128 * t:128 * (t + 1)],
                                    in0=ps_otM[:, 128 * u:128 * (u + 1)],
                                    scalar1=csc, scalar2=None, op0=OP.mult)
                        vector.drain()
                        VV.inc(vector.engine_nop())
                    strip_vv.append(VV.n)
                    it += 1
                dslab_used[hc % 3] = MM.n

                @block.gpsimd
                def _(gpsimd, e=e, hc=hc):
                    VV.wait(gpsimd)
                    if e > 0 and acc_prev is not None:
                        ACC.wait(gpsimd, acc_prev[hc])
                    ov = out_d.rearrange("(tt tp) h -> tp tt h", tp=128)
                    ins = gpsimd.dma_start(
                        out=ov[:, :, 128 * hc:128 * (hc + 1)],
                        in_=ostrip[:].rearrange("p (tt h) -> p tt h", tt=NT),
                        **({"accum_op": OP.add} if e > 0 else {}))
                    ACC.incd(ins)
                prev_store_at = ACC.n
            acc_prev = {hc: ACC.n - 16 * (ND - 1 - hc) for hc in range(ND)}
            dn_mm_prev = MM.n
        phE.close()
        phD.close()
    else:
        src = {1: x1_s, 2: x2_s, 3: x3_s}[upto]

        @block.sync
        def _(sync):
            ST.wait(sync)
            STG.wait(sync)
            ST.incd(sync.dma_start(out=out_d, in_=src))

    if taps:
        @block.sync
        def _(sync):
            ST.wait(sync)
            STG.wait(sync)
            ACC.wait(sync)
            for tnm in taps:
                ST.incd(sync.dma_start(out=tap_out[tnm], in_=tapt[tnm]))

    @block.sync
    def _(sync):
        ST.wait(sync)
        STG.wait(sync)
        ACC.wait(sync)

    ctx.close()
    return nc


def _host_consts():
    ident = np.eye(128, dtype=np.float32).astype(ml_dtypes.bfloat16)
    iota2 = np.broadcast_to(np.arange(S, dtype=np.float32), (2, S)).copy()
    init4 = np.ones((4, S), dtype=np.float32).astype(ml_dtypes.bfloat16)
    jj = np.arange(128)
    mqk = np.where(jj[None, :] > jj[:, None], NEG, 0.0).astype(np.float32)
    mkq = np.where(jj[:, None] > jj[None, :], NEG, 0.0).astype(np.float32)
    onesb = np.ones((1, 512), dtype=np.float32).astype(ml_dtypes.bfloat16)
    onesc = np.ones((128, 1), dtype=np.float32).astype(ml_dtypes.bfloat16)
    return dict(ident=ident, iota2=iota2, init4=init4, mqk=mqk, mkq=mkq,
                onesb=onesb, onesc=onesc)


def prepare_maps(inputs):
    x = np.ascontiguousarray(np.asarray(inputs["L0_attn_x"], dtype=np.float32))

    def row(a):
        return np.ascontiguousarray(np.asarray(a, np.float32).reshape(1, -1))

    def colsq(a, ncol):
        return np.ascontiguousarray(np.asarray(a, np.float32).reshape(ncol, 128).T)

    w2 = np.zeros((16, 128, 384), dtype=np.float32)
    wqkv = np.asarray(inputs["L1_attn_Wqkv"], np.float32)
    for p in range(16):
        w2[p, 0:64, 0:192] = wqkv[2 * p]
        w2[p, 64:128, 192:384] = wqkv[2 * p + 1]
    bqkv2 = np.ascontiguousarray(
        np.asarray(inputs["L1_attn_bqkv"], np.float32).reshape(1, -1))
    wout = np.asarray(inputs["L1_attn_Wout"], np.float32)
    woutbd = np.zeros((128, 128), dtype=np.float32)
    woutbd[0:64, 0:64] = wout
    woutbd[64:128, 64:128] = wout
    bout2 = np.ascontiguousarray(
        np.tile(np.asarray(inputs["L1_attn_bout"], np.float32), 2).reshape(1, -1))

    common = dict(
        wq=np.ascontiguousarray(np.asarray(inputs["L0_attn_wq"], np.float32)),
        wk=np.ascontiguousarray(np.asarray(inputs["L0_attn_wk"], np.float32)),
        wv=np.ascontiguousarray(np.asarray(inputs["L0_attn_wv"], np.float32)),
        wo=np.ascontiguousarray(np.asarray(inputs["L0_attn_wo"], np.float32)),
        bq=row(inputs["L0_attn_bq"]), bk=row(inputs["L0_attn_bk"]),
        bv=row(inputs["L0_attn_bv"]), bo=row(inputs["L0_attn_bo"]),
        gw=np.ascontiguousarray(np.asarray(inputs["L0_ffn_gate_w"], np.float32)),
        uw=np.ascontiguousarray(np.asarray(inputs["L0_ffn_up_w"], np.float32)),
        dw=np.ascontiguousarray(np.asarray(inputs["L0_ffn_down_w"], np.float32)),
        gs=colsq(inputs["L0_ffn_gate_scale"], NI0),
        us=colsq(inputs["L0_ffn_up_scale"], NI0),
        ds=colsq(inputs["L0_ffn_down_scale"], ND),
        w2=w2, bqkv2=bqkv2, woutbd=woutbd, bout2=bout2,
        rw=np.ascontiguousarray(np.asarray(inputs["L1_ffn_router_w"], np.float32)),
        mgw=np.ascontiguousarray(
            np.asarray(inputs["L1_ffn_gate_w"], np.float32).transpose(0, 2, 1)),
        muw=np.ascontiguousarray(
            np.asarray(inputs["L1_ffn_up_w"], np.float32).transpose(0, 2, 1)),
        mdw=np.ascontiguousarray(
            np.asarray(inputs["L1_ffn_down_w"], np.float32).transpose(0, 2, 1)),
        **_host_consts(),
    )
    return common, x


def kernel(**inputs):
    common, x = prepare_maps(inputs)
    nc = build_program()
    in_maps = [dict(common, x0=np.ascontiguousarray(x[c])) for c in range(8)]
    res = run_bass_kernel_spmd(nc, in_maps, core_ids=list(range(8)))
    return np.stack([np.asarray(res.results[c]["out"]) for c in range(8)], axis=0)
